# version 14
# baseline (speedup 1.0000x reference)
import numpy as np
import concourse.bass as bass
import concourse.mybir as mybir
from concourse.bass_utils import run_bass_kernel_spmd

F32 = mybir.dt.float32
BF16 = mybir.dt.bfloat16
AF = mybir.ActivationFunctionType
ALU = mybir.AluOpType
AX = mybir.AxisListType

D = 1024
RING = 8
DVE_DRAIN = True
DT_B = {F32: 4, BF16: 2}
GROUPS = [list(range(4 * g, 4 * g + 4)) for g in range(5)]
NSLOT = 6


class Sem:
    def __init__(self, nc, name):
        self.h = nc.alloc_semaphore(name)
        self.n = 0


def build_nc(S=4096):
    NT = S // 128
    NST = S // 512
    NFT = S // 512

    nc = bass.Bass("TRN2", target_bir_lowering=False)

    din = lambda n, s: nc.dram_tensor(n, list(s), F32, kind="ExternalInput").ap()
    xT = din("xT", [D, S])
    xl_d = din("xl", [8, 2 * D])
    alpha_d = din("alpha", [1, 1])
    wq_d, wk_d, wv_d, wfc_d = (din(n, [D, D]) for n in ("wq", "wk", "wv", "wfc"))
    bq_d, bk_d, bfc_d, bv_d = (din(n, [128, 8]) for n in ("bq", "bk", "bfc", "bv"))
    wo_d = din("wo2", [128, 1024])
    rw_d = din("rw", [D, 16])
    wg_r = din("wg_r", [16, D, 256])
    wu_r = din("wu_r", [16, D, 256])
    wd_r = din("wd_r", [16, 256, D])
    wg_s = din("wg_s", [2, D, 512])
    wu_s = din("wu_s", [2, D, 512])
    wd_s = din("wd_s", [2, 512, D])
    outT = nc.dram_tensor("outT", [D, S], F32, kind="ExternalOutput").ap()
    facc = nc.dram_tensor("facc", [8, 128, S], BF16, kind="Internal").ap()

    SB_BASE = 17408
    cur = [SB_BASE]

    def A(name, shape, dtype):
        nb = int(np.prod(shape[1:])) * DT_B[dtype]
        off = (cur[0] + 63) // 64 * 64
        t = nc.alloc_sbuf_tensor_at(name, list(shape), dtype, offset=off)
        cur[0] = off + nb
        return t

    x_bf = A("x_bf", [128, 8 * S], BF16)
    mask_mid = A("mask_mid", [128, 768], BF16)
    maskb_e0 = A("maskb_e0", [128, 768], BF16)
    maskb_e1 = A("maskb_e1", [128, 768], BF16)
    ones128 = A("ones128", [128, 128], BF16)
    ident = A("ident", [128, 128], BF16)
    ones1r = A("ones1r", [1, 128], BF16)
    ones8f = A("ones8f", [8, 128], F32)
    ones1f = A("ones1f", [1, 128], F32)
    bq8 = A("bq8", [128, 8], F32)
    bk_s = A("bk_s", [128, 8], F32)
    bfc_s = A("bfc_s", [128, 8], F32)
    bv_s = A("bv_s", [128, 8], F32)
    hh_sb = A("hh_sb", [128, 2], F32)
    sct = A("sct", [128, 8], F32)
    alpha_s = A("alpha_s", [1, 1], F32)
    eps_sb = A("eps_sb", [128, 1], F32)
    negmax = A("negmax", [128, 8], F32)
    den = A("den", [128, 16], F32)
    rden = A("rden", [128, 16], F32)
    mu_bc = A("mu_bc", [128, 512], BF16)
    var_t = A("var_t", [128, 512], F32)
    r_bf = A("r_bf", [128, 512], BF16)
    sq_bf = A("sq_bf", [128, 1024], BF16)
    rw_sb = A("rw_sb", [128, 8 * 16], BF16)
    PERS_END = cur[0]

    OV = (PERS_END + 63) // 64 * 64
    cur[0] = OV
    wq_sb = A("wq_sb", [128, 8 * 1024], BF16)
    wk_sb = A("wk_sb", [128, 8 * 1024], BF16)
    wv_sb = A("wv_sb", [128, 8 * 1024], BF16)
    wfc_sb = A("wfc_sb", [128, 8 * 1024], BF16)
    wo_sb = A("wo_sb", [128, 1024], BF16)
    k_ring = A("k_ring", [128, 8 * RING * 128], BF16)
    v_ring = A("v_ring", [128, RING * 1024], BF16)
    q_bf = A("q_bf", [128, 2 * 8 * 512], BF16)
    E_sb = A("E_sb", [128, 4 * 768], BF16)
    AT_sb = A("AT_sb", [128, 3 * 768], BF16)
    ao_bf = A("ao_bf", [128, 2 * 128], BF16)
    out2_bf = A("out2_bf", [128, 8 * 256], BF16)
    ATTN_END = cur[0]

    cur[0] = OV
    eslot = [A(f"eslot{i}", [128, 6144], BF16) for i in range(NSLOT)]
    combT = A("combT", [16, S], BF16)
    comb_rowT = A("comb_rowT", [1, 2 * 4 * 512], BF16)
    comb_bc = A("comb_bc", [128, 2 * 512], BF16)
    silu_sb = A("silu_sb", [128, 1024], BF16)
    h_sb = A("h_sb", [128, 4 * 1024], BF16)
    stage = A("stage", [128, 2 * 512], BF16)
    acc_ld = A("acc_ld", [128, 2 * 8 * 512], BF16)
    lg_sb = A("lg_sb", [128, 64], F32)
    t2a = A("t2a", [128, 32], F32)
    t2b = A("t2b", [128, 64], F32)
    comb_sb = A("comb_sb", [128, 64], F32)
    comb_bf = A("comb_bf", [128, 64], BF16)
    out_st2 = A("out_st2", [128, 2 * 512], F32)
    FFN_END = cur[0]

    LIMIT = 229216
    assert max(ATTN_END, FFN_END) <= LIMIT, (PERS_END, ATTN_END, FFN_END)

    klv = k_ring[:, :].bitcast(F32)
    xl_t = klv[0:8, 0:2 * D]
    d8_t = klv[0:8, 2 * D:3 * D]
    ds_t = klv[0:8, 3 * D:3 * D + 1]
    assert 3 * D + 1 <= RING * 8 * 128 // 2
    DIFF = AT_sb[:, :].bitcast(F32)[:, 0:768]

    P01 = nc.alloc_psum_tensor("P01", [128, 1024], F32)
    P23 = nc.alloc_psum_tensor("P23", [128, 1024], F32)
    P4 = nc.alloc_psum_tensor("P4", [128, 1024], BF16)
    P5 = nc.alloc_psum_tensor("P5", [128, 512], F32)
    P6 = nc.alloc_psum_tensor("P6", [128, 512], F32)
    P7 = nc.alloc_psum_tensor("P7", [128, 512], F32)

    ENGS = ("te", "dve", "act", "pool", "sync")
    P = {e: [] for e in ENGS}
    seen = {e: {} for e in ENGS}

    def sems(*names):
        return [Sem(nc, n) for n in names]

    (d_x, d_w, d_ld, d_out2, d_acc2, d_cr2, d_fa2) = (
        sems("d_x", "d_w", "d_ld")
        + [sems("d_outa", "d_outb"), sems("d_aca", "d_acb"),
           sems("d_cra", "d_crb"), sems("d_faa", "d_fab")])
    d_es = sems(*[f"d_es{i}" for i in range(NSLOT)])
    (s_pj, s_pjev, s_sc, s_mx, s_ex, s_A, s_tr, s_atev, s_av, s_aoev, s_wo,
     s_o2ev, s_wfc, s_s1, s_stat, s_lnd, s_lna, s_hh, s_msk, s_ri) = sems(
        "s_pj", "s_pjev", "s_sc", "s_mx", "s_ex", "s_A", "s_tr", "s_atev",
        "s_av", "s_aoev", "s_wo", "s_o2ev", "s_wfc", "s_s1", "s_stat",
        "s_lnd", "s_lna", "s_hh", "s_msk", "s_ri")
    (s_dva, s_dvb) = sems("s_dva", "s_dvb")
    (s_rt, s_rtd, s_rta, s_g, s_u, s_cb, s_cbe, s_sil, s_usc, s_h, s_dn,
     s_dne, s_grp, s_s2, s_x2) = sems(
        "s_rt", "s_rtd", "s_rta", "s_g", "s_u", "s_cb", "s_cbe", "s_sil",
        "s_usc", "s_h", "s_dn", "s_dne", "s_grp", "s_s2", "s_x2")

    def op(eng, fn, inc=None, inc_n=1, waits=(), drain=False):
        w = []
        for (s, v) in waits:
            if v > 0 and seen[eng].get(s, -1) < v:
                seen[eng][s] = v
                w.append((s.h, v))
        if drain and (eng != "dve" or DVE_DRAIN):
            P[eng].append(lambda e: e.drain())

        def g(e, fn=fn, w=tuple(w), inc=inc, inc_n=inc_n):
            for (h, v) in w:
                e.wait_ge(h, v)
            r = fn(e)
            if inc is not None:
                r.then_inc(inc.h, inc_n)
        P[eng].append(g)
        if inc is not None:
            inc.n += inc_n
        return inc.n if inc is not None else None

    def dma(eng, out, in_, inc, waits=(), accum=False, maxdim=None):
        def f(e, out=out, in_=in_, accum=accum, maxdim=maxdim):
            kw = {}
            if maxdim is not None:
                kw["max_dma_last_dim"] = maxdim
            if accum:
                return e.dma_start(out=out, in_=in_, accum_op=ALU.add, **kw)
            return e.dma_start(out=out, in_=in_, **kw)
        return op(eng, f, inc=inc, inc_n=16, waits=waits)

    rel = {}

    def claim(res_list):
        w = []
        for r in res_list:
            if r in rel:
                w.append(rel[r])
        return w

    def release(res_list, sem):
        for r in res_list:
            rel[r] = (sem, sem.n)
    ROT = [(P01, 0), (P01, 512), (P23, 0), (P23, 512)]
    rot_i = [0]

    def dma_x(sx, w_):
        dma("pool",
            x_bf[:, :].rearrange("p (c t) -> p c t", c=8)[:, :, sx * w_:
                                                          (sx + 1) * w_],
            xT.rearrange("(c p) t -> p c t", p=128)[:, :, sx * w_:
                                                    (sx + 1) * w_], d_x)

    def dma_w(dst, src):
        dma("pool", dst[:, :].rearrange("p (c f) -> p c f", c=8),
            src.rearrange("(c p) f -> p c f", p=128), d_w)

    XW = S // 4
    dma_w(wq_sb, wq_d)
    WQ_N = d_w.n
    dma_x(0, XW)
    dma_w(wk_sb, wk_d)
    WK_N = d_w.n
    dma_w(wv_sb, wv_d)
    WV_N = d_w.n
    dma_x(1, XW)
    dma_x(2, XW)
    dma_w(wfc_sb, wfc_d)
    WFC_N = d_w.n
    dma_x(3, XW)
    D_X_ALL = d_x.n
    X_N = [16, 32, 48, 64]
    dma("pool", wo_sb[:, :], wo_d, d_w)
    WO_N = d_w.n
    dma("pool", rw_sb[:, :].rearrange("p (c f) -> p c f", c=8),
        rw_d.rearrange("(c p) f -> p c f", p=128), d_w)
    W_BASE = d_w.n

    for (dst, src) in ((bq8, bq_d), (bk_s, bk_d), (bfc_s, bfc_d), (bv_s, bv_d),
                       (alpha_s, alpha_d)):
        dma("sync", dst[:, :], src, d_ld)
    dma("sync", xl_t, xl_d, d_ld)
    LD_ALL = d_ld.n

    op("dve", lambda e: e.memset(ones128[:, :], 1.0))
    op("dve", lambda e: e.memset(ones1r[:, :], 1.0))
    op("dve", lambda e: e.memset(ones8f[:, :], 1.0))
    op("dve", lambda e: e.memset(ones1f[:, :], 1.0))
    op("dve", lambda e: e.memset(eps_sb[:, :], 1e-5))
    op("dve", lambda e: e.memset(v_ring[:, :], 0.0), inc=s_ri)
    op("pool", lambda e: e.iota(ident[:, :], pattern=[[-1, 128]], base=0,
                                channel_multiplier=1,
                                allow_small_or_imprecise_dtypes=True),
       inc=s_msk)
    op("pool", lambda e: e.iota(DIFF, pattern=[[0, 2], [-1, 384]], base=128,
                                channel_multiplier=1,
                                allow_small_or_imprecise_dtypes=True),
       inc=s_msk)
    op("dve", lambda e: e.tensor_single_scalar(ident[:, :], ident[:, :], 0.0,
                                               op=ALU.is_equal),
       waits=[(s_msk, 1)], inc=s_ri, drain=True)

    op("dve", lambda e: e.tensor_sub(d8_t, xl_t[:, 0:D], xl_t[:, D:2 * D]),
       waits=[(d_ld, LD_ALL)], inc=s_hh, drain=True)
    op("act", lambda e: e.activation(d8_t, d8_t, AF.Abs),
       waits=[(s_hh, 1)], inc=s_hh)
    op("dve", lambda e: e.reduce_sum(ds_t, d8_t, axis=AX.X),
       waits=[(s_hh, 2)], inc=s_hh, drain=True)
    hw = claim(["p7"])
    op("te", lambda e: e.matmul(P7[:, 0:1], ones8f[:, :], ds_t, start=True,
                                stop=True),
       waits=[(s_hh, 3), (s_ri, 2)] + hw)
    op("te", lambda e: e.matmul(P7[:, 1:2], ones1f[:, :], d8_t[0:1, 0:1],
                                start=True, stop=True))
    op("te", lambda e: e.matmul(P7[:, 2:3], ones1f[:, :], alpha_s[0:1, 0:1],
                                start=True, stop=True),
       waits=[(d_ld, LD_ALL)], inc=s_hh)
    op("dve", lambda e: e.tensor_single_scalar(sct[:, 0:1], P7[:, 0:1],
                                               1.0 / (8 * D), op=ALU.mult),
       waits=[(s_hh, 4)], drain=True)
    op("act", lambda e: e.activation(sct[:, 1:2], P7[:, 2:3], AF.Sigmoid),
       waits=[(s_hh, 4)], inc=s_hh)
    op("dve", lambda e: e.tensor_sub(sct[:, 2:3], sct[:, 0:1], P7[:, 1:2]),
       drain=True)
    op("dve", lambda e: e.tensor_mul(sct[:, 2:3], sct[:, 1:2], sct[:, 2:3]),
       waits=[(s_hh, 5)], drain=True)
    op("dve", lambda e: e.tensor_add(sct[:, 2:3], sct[:, 2:3], P7[:, 1:2]),
       inc=s_hh, drain=True)
    release(["p7"], s_hh)
    op("act", lambda e: e.activation(sct[:, 3:4], sct[:, 2:3], AF.Sigmoid),
       waits=[(s_hh, 6)], inc=s_hh, drain=True)
    op("dve", lambda e: e.tensor_scalar(hh_sb[:, 0:1], sct[:, 3:4], 48.0, 12.0,
                                        op0=ALU.mult, op1=ALU.add),
       waits=[(s_hh, 7)], drain=True)
    op("dve", lambda e: e.tensor_single_scalar(hh_sb[:, 1:2], hh_sb[:, 0:1],
                                               -1.0, op=ALU.mult),
       inc=s_hh, drain=True)
    m1s = E_sb[:, 0:768]
    op("dve", lambda e: e.tensor_single_scalar(m1s, DIFF, hh_sb[:, 0:1],
                                               op=ALU.is_le),
       waits=[(s_hh, 8), (s_msk, 2)], drain=True)
    op("dve", lambda e: e.tensor_single_scalar(mask_mid[:, :], DIFF,
                                               hh_sb[:, 1:2], op=ALU.is_ge),
       drain=True)
    op("dve", lambda e: e.tensor_mul(mask_mid[:, :], mask_mid[:, :], m1s),
       drain=True)
    op("dve", lambda e: e.tensor_scalar(mask_mid[:, :], mask_mid[:, :], 100.0,
                                        -100.0, op0=ALU.mult, op1=ALU.add),
       drain=True)
    op("dve", lambda e: e.tensor_copy(maskb_e0[:, :], mask_mid[:, :]),
       drain=True)
    op("dve", lambda e: e.tensor_copy(maskb_e1[:, :], mask_mid[:, :]),
       drain=True)
    op("dve", lambda e: e.memset(
        maskb_e0[:, :].rearrange("p (h w) -> p h w", h=2)[:, :, 0:128], -100.0),
       drain=True)
    op("dve", lambda e: e.memset(
        maskb_e1[:, :].rearrange("p (h w) -> p h w", h=2)[:, :, 256:384],
        -100.0), inc=s_msk, drain=True)
    MSK_DONE = s_msk.n
    op("dve", lambda e: e.memset(k_ring[:, :], 0.0),
       waits=[(s_hh, 4)], inc=s_ri, drain=True)
    RI_DONE = s_ri.n
    op("dve", lambda e: e.tensor_single_scalar(bq8[:, :], bq8[:, :], 0.125,
                                               op=ALU.mult),
       waits=[(d_ld, LD_ALL)], inc=s_hh, drain=True)
    HH_DONE = s_hh.n

    PROJQ = []

    def proj_set(mk_mm, evacs, extra_waits=()):
        PROJQ.append((mk_mm, evacs, list(extra_waits)))

    def emit_proj_set():
        if not PROJQ:
            return
        mk_mm, evacs, extra = PROJQ.pop(0)
        ri = rot_i[0] % 4
        rot_i[0] += 1
        ps_t, ps_o = ROT[ri]
        ps = ps_t[:, ps_o:ps_o + 512]
        w = claim([f"r{ri}"]) + extra
        for i in range(8):
            op("te", (lambda e, i=i, ps=ps: mk_mm(e, i, ps)),
               waits=(w if i == 0 else ()), inc=(s_pj if i == 7 else None))
        mmv = s_pj.n
        for ev in evacs:
            op("act", (lambda e, ev=ev, ps=ps: ev(e, ps)),
               waits=[(s_pj, mmv)], inc=s_pjev)
        release([f"r{ri}"], s_pjev)

    def drain_projq():
        while PROJQ:
            emit_proj_set()

    q_ready = {}
    k_ready = {}

    def project(st):
        base_w = [(d_x, X_N[st // 2]), (s_ri, RI_DONE), (s_hh, HH_DONE)]
        for fc in range(8):
            def mk(e, i, ps, fc=fc):
                return e.matmul(ps, wq_sb[:, i * 1024 + fc * 128:
                                          i * 1024 + fc * 128 + 128],
                                x_bf[:, i * S + st * 512:i * S + st * 512 + 512],
                                start=(i == 0), stop=(i == 7))

            def ev(e, ps, fc=fc):
                return e.activation(
                    q_bf[:, (st % 2) * 4096 + fc * 512:
                         (st % 2) * 4096 + fc * 512 + 512],
                    ps, AF.Identity, bias=bq8[:, fc:fc + 1], scale=0.125)
            proj_set(mk, [ev], extra_waits=base_w + [(d_w, WQ_N)])
        for fc in range(8):
            def mk(e, i, ps, fc=fc):
                return e.matmul(ps, wk_sb[:, i * 1024 + fc * 128:
                                          i * 1024 + fc * 128 + 128],
                                x_bf[:, i * S + st * 512:i * S + st * 512 + 512],
                                start=(i == 0), stop=(i == 7))
            evs = []
            for tt in range(4):
                slot = (4 * st + tt) % RING

                def ev(e, ps, fc=fc, tt=tt, slot=slot):
                    return e.activation(
                        k_ring[:, fc * RING * 128 + slot * 128:
                               fc * RING * 128 + slot * 128 + 128],
                        ps[:, tt * 128:tt * 128 + 128], AF.Identity,
                        bias=bk_s[:, fc:fc + 1], scale=1.0)
                evs.append(ev)
            proj_set(mk, evs, extra_waits=base_w + [(d_w, WK_N)])
        for tt in range(4):
            T = 4 * st + tt
            slot = T % RING
            for fh in range(2):
                def mk(e, i, ps, T=T, fh=fh):
                    return e.matmul(ps, x_bf[:, i * S + T * 128:
                                             i * S + T * 128 + 128],
                                    wv_sb[:, i * 1024 + fh * 512:
                                          i * 1024 + fh * 512 + 512],
                                    start=(i == 0), stop=(i == 7))

                def ev(e, ps, slot=slot, fh=fh):
                    return e.copy(v_ring[:, slot * 1024 + fh * 512:
                                         slot * 1024 + fh * 512 + 512], ps)
                proj_set(mk, [ev], extra_waits=base_w + [(d_w, WV_N)])
        done = s_pjev.n + sum(len(ev) for (_, ev, _) in PROJQ)
        q_ready[st % 2] = done
        for tt in range(4):
            k_ready[4 * st + tt + 1] = done
        k_ready[0] = k_ready.get(0, done)

    SCRES = [["r0", "r1"], ["r2", "r3"]]
    SC = [P01, P23]

    ATTST = {}
    GP = [0]
    E_LAST = {}
    AT_LAST = {}

    def emit_round(new_s, l1, l2, l3):
        if new_s is not None:
            s = new_s
            p, scp, par, T = s["p"], s["scp"], s["par"], s["T"]
            w0 = claim(SCRES[s["g"] % 2]) + [
                (s_pjev, q_ready[par]), (s_pjev, k_ready[min(T + 1, NT)]),
                (s_msk, MSK_DONE)]
            runs = []
            c = 0
            while c < 3:
                slot = (T - 1 + c) % RING
                w_ = 1
                while c + w_ < 3 and (T - 1 + c + w_) % RING == slot + w_:
                    w_ += 1
                runs.append((c, slot, w_))
                c += w_
            nmm = 2 * len(runs)
            im = 0
            for hi in range(2):
                b = hi * 64
                for (c0, slot, w_) in runs:
                    def mk(e, c0=c0, b=b, slot=slot, w_=w_, p=p, scp=scp,
                           par=par, T=T):
                        return e.matmul(
                            scp[:, (b // 64) * 512 + c0 * 128:
                                (b // 64) * 512 + c0 * 128 + 128 * w_],
                            q_bf[b:b + 64, par * 4096 + p * 512 + (T % 4) * 128:
                                 par * 4096 + p * 512 + (T % 4) * 128 + 128],
                            k_ring[b:b + 64, p * RING * 128 + slot * 128:
                                   p * RING * 128 + slot * 128 + 128 * w_],
                            start=True, stop=True)
                    im += 1
                    op("te", mk, waits=(w0 if im == 1 else ()),
                       inc=(s_sc if im == nmm else None))
            s["scv"] = s_sc.n
            segs = scp[:, :].rearrange("q (h w) -> q h w", h=2)[:, :, 0:384]
            s["segs"] = segs
        if l1 is not None:
            s = l1
            op("dve", lambda e, s=s: e.reciprocal(
                   rden[:, 2 * s["p8"]:2 * s["p8"] + 2],
                   den[:, 2 * s["p8"]:2 * s["p8"] + 2]),
               waits=[(s_ex, s["exv"])], inc=s_dvb)
            s["dvb"] = s_dvb.n
            op("dve", lambda e, s=s: e.tensor_single_scalar(
                   s["E"][:, 0:384], s["E"][:, 0:384],
                   rden[:, 2 * s["p8"]:2 * s["p8"] + 1], op=ALU.mult),
               waits=[(s_dvb, s["dvb"])])
            op("dve", lambda e, s=s: e.tensor_single_scalar(
                   s["E"][:, 384:768], s["E"][:, 384:768],
                   rden[:, 2 * s["p8"] + 1:2 * s["p8"] + 2], op=ALU.mult),
               inc=s_A)
            s["avl"] = s_A.n
            tw = claim(["p4"]) + [(s_A, s["avl"]), (s_ri, 2)]
            for c in range(6):
                op("te", lambda e, s=s, c=c: e.transpose(
                       P4[:, c * 128:c * 128 + 128],
                       s["E"][:, c * 128:c * 128 + 128], ident[:, :]),
                   waits=(tw if c == 0 else ()),
                   inc=(s_tr if c == 5 else None))
            s["trv"] = s_tr.n
            E_LAST[s["g"] % 4] = s["trv"]
        if l3 is not None:
            s = l3
            o2w = [(s_wo, s["wov"]), (s_wfc, s_wfc.n)]
            op("act", lambda e, s=s: e.copy(
                   out2_bf[:, s["o2c"]:s["o2c"] + 128], P7[:, 0:128]),
               waits=o2w, inc=s_o2ev)
            release(["p7"], s_o2ev)
        if l1 is not None:
            s = l1
            atw = [(s_tr, s["trv"])]
            if s["g"] % 3 in AT_LAST:
                atw.append((s_av, AT_LAST[s["g"] % 3]))
            op("act", lambda e, s=s: e.copy(s["AT"], P4[:, 0:768]),
               waits=atw, inc=s_atev)
            s["atv"] = s_atev.n
            release(["p4"], s_atev)
            lo = (s["g"] % 2 == 0)
            aw = claim(["p5" if lo else "p6"]) + [(s_atev, s["atv"])]
            for c in range(3):
                slot = (s["T"] - 1 + c) % RING
                for hi in range(2):
                    def mk(e, c=c, hi=hi, slot=slot, s=s, lo=lo):
                        dst = (P5[hi * 64:hi * 64 + 64, 0:128] if lo
                               else P6[hi * 64:hi * 64 + 64, 128:256])
                        return e.matmul(
                            dst,
                            v_ring[:, slot * 1024 + (2 * s["p"] + hi) * 64:
                                   slot * 1024 + (2 * s["p"] + hi) * 64 + 64],
                            s["AT"][:, hi * 384 + c * 128:
                                    hi * 384 + c * 128 + 128],
                            start=(c == 0), stop=(c == 2),
                            tile_position=(0, hi * 64), skip_group_check=True)
                    op("te", mk, waits=(aw if (c == 0 and hi == 0) else ()),
                       inc=(s_av if (c == 2 and hi == 1) else None))
            s["avv"] = s_av.n
            AT_LAST[s["g"] % 3] = s["avv"]
        if l2 is not None:
            s = l2
            lo = (s["g"] % 2 == 0)
            src_ps = P5[:, 0:128] if lo else P6[:, 128:256]
            op("act", lambda e, s=s, src_ps=src_ps: e.activation(
                   s["ao"], src_ps, AF.Identity,
                   bias=bv_s[:, s["p8"]:s["p8"] + 1], scale=1.0),
               waits=[(s_av, s["avv"])], inc=s_aoev)
            s["aov"] = s_aoev.n
            release(["p5" if (s["g"] % 2 == 0) else "p6"], s_aoev)
            ww = claim(["p7"]) + [(s_aoev, s["aov"]), (d_w, WO_N)]
            op("te", lambda e, s=s: e.matmul(
                   P7[:, 0:128],
                   wo_sb[:, s["p"] * 128:s["p"] * 128 + 128],
                   s["ao"][:, :], start=True, stop=True),
               waits=ww, inc=s_wo)
            s["wov"] = s_wo.n
        if new_s is not None:
            s = new_s
            op("dve", lambda e, segs=s["segs"], mb=s["mb"]:
                   e.scalar_tensor_tensor(segs, segs, 1.0, mb[:, :],
                                          op0=ALU.mult, op1=ALU.add),
               waits=[(s_sc, s["scv"])], inc=s_dva)
            s["dva"] = s_dva.n
            op("dve", lambda e, s=s: e.reduce_max(
                   negmax[:, s["p8"]:s["p8"] + 1], s["segs"], axis=AX.XY,
                   negate=True),
               waits=[(s_dva, s["dva"])], inc=s_mx)
            s["mxv"] = s_mx.n
            ew = []
            if s["g"] % 4 in E_LAST:
                ew = [(s_tr, E_LAST[s["g"] % 4])]
            op("act", lambda e, s=s: e.activation(
                   s["E"][:, 0:384], s["scp"][:, 0:384], AF.Exp,
                   bias=negmax[:, s["p8"]:s["p8"] + 1],
                   accum_out=den[:, 2 * s["p8"]:2 * s["p8"] + 1]),
               waits=[(s_mx, s["mxv"])] + ew)
            op("act", lambda e, s=s: e.activation(
                   s["E"][:, 384:768], s["scp"][:, 512:896], AF.Exp,
                   bias=negmax[:, s["p8"]:s["p8"] + 1],
                   accum_out=den[:, 2 * s["p8"] + 1:2 * s["p8"] + 2]),
               inc=s_ex)
            s["exv"] = s_ex.n
            release(SCRES[s["g"] % 2], s_ex)
    def att_step(new_s):
        g = GP[0]
        l1 = ATTST.get(g - 1)
        l2 = ATTST.get(g - 2)
        l3 = ATTST.get(g - 3)
        emit_round(new_s, l1, l2, l3)
        if new_s is not None:
            ATTST[g] = new_s
            GP[0] += 1

    def attend_tile(T, st):
        par = st % 2
        maskb = mask_mid if (0 < T < NT - 1) else (maskb_e0 if T == 0
                                                   else maskb_e1)
        for p in range(8):
            g = GP[0]
            s = {"p": p, "p8": p, "T": T, "par": par, "g": g, "mb": maskb,
                 "scp": SC[g % 2],
                 "E": E_sb[:, (g % 4) * 768:(g % 4) * 768 + 768],
                 "AT": AT_sb[:, (g % 3) * 768:(g % 3) * 768 + 768],
                 "ao": ao_bf[:, (g % 2) * 128:(g % 2) * 128 + 128],
                 "o2c": p * 256 + (T % 2) * 128}
            att_step(s)

    def att_flush():
        for _ in range(3):
            g = GP[0]
            emit_round(None, ATTST.get(g - 1), ATTST.get(g - 2),
                       ATTST.get(g - 3))
            GP[0] += 1

    def wfc_ln(st, half):
        tok0 = st * 512 + half * 256
        o2v = s_o2ev.n
        chunks = []
        for fc in range(8):
            ri = rot_i[0] % 4
            rot_i[0] += 1
            ps_t, ps_o = ROT[ri]
            ps = ps_t[:, ps_o:ps_o + 256]
            w0 = claim([f"r{ri}"]) + [(s_o2ev, o2v), (d_w, WFC_N)]
            for i in range(8):
                def mk(e, i=i, fc=fc, ps=ps):
                    return e.matmul(ps, wfc_sb[:, i * 1024 + fc * 128:
                                               i * 1024 + fc * 128 + 128],
                                    out2_bf[:, i * 256:i * 256 + 256],
                                    start=(i == 0), stop=(i == 7))
                op("te", mk, waits=(w0 if i == 0 else ()),
                   inc=(s_wfc if i == 7 else None))
            mmv = s_wfc.n
            xsl = x_bf[:, fc * S + tok0:fc * S + tok0 + 256]
            op("dve", lambda e, ps=ps, xsl=xsl, fc=fc: e.scalar_tensor_tensor(
                   xsl, ps, bfc_s[:, fc:fc + 1], xsl, op0=ALU.add, op1=ALU.add),
               waits=[(s_wfc, mmv)], inc=s_s1, drain=True)
            release([f"r{ri}"], s_s1)
            chunks.append((fc, s_s1.n, xsl))
        sw = claim(["p5", "p6"])
        st_base = s_stat.n
        for j, (fc, s1v, xsl) in enumerate(chunks):
            op("te", lambda e, xsl=xsl, j=j: e.matmul(
                   P5[:, 0:256], ones128[:, :], xsl, start=(j == 0),
                   stop=(j == 7)),
               waits=([(s_s1, s1v)] + sw if j == 0 else [(s_s1, s1v)]),
               inc=s_stat)
        sq_v = []
        for j, (fc, s1v, xsl) in enumerate(chunks):
            sqs = sq_bf[:, (j % 4) * 256:(j % 4) * 256 + 256]
            w = [(s_s1, s1v)]
            if j >= 4:
                w.append((s_stat, st_base + 8 + (j - 4) + 1))
            op("act", lambda e, xsl=xsl, sqs=sqs: e.activation(sqs, xsl,
                                                               AF.Square),
               waits=w, inc=s_lna, drain=True)
            sq_v.append(s_lna.n)
        for j in range(8):
            sqs = sq_bf[:, (j % 4) * 256:(j % 4) * 256 + 256]
            op("te", lambda e, j=j, sqs=sqs: e.matmul(
                   P6[:, 0:256], ones128[:, :], sqs, start=(j == 0),
                   stop=(j == 7)),
               waits=[(s_lna, sq_v[j])], inc=s_stat)
        stv = s_stat.n
        op("dve", lambda e: e.tensor_single_scalar(mu_bc[:, 0:256], P5[:, 0:256],
                                                   1.0 / D, op=ALU.mult),
           waits=[(s_stat, stv)], drain=True)
        op("dve", lambda e: e.tensor_mul(var_t[:, 0:256], mu_bc[:, 0:256],
                                         mu_bc[:, 0:256]), drain=True)
        op("dve", lambda e: e.scalar_tensor_tensor(
               var_t[:, 0:256], P6[:, 0:256], 1.0 / D, var_t[:, 0:256],
               op0=ALU.mult, op1=ALU.subtract), inc=s_lnd, drain=True)
        lv = s_lnd.n
        release(["p5", "p6"], s_lnd)
        op("act", lambda e: e.activation(var_t[:, 0:256], var_t[:, 0:256],
                                         AF.Ln, bias=eps_sb[:, 0:1]),
           waits=[(s_lnd, lv)], drain=True)
        op("act", lambda e: e.activation(r_bf[:, 0:256], var_t[:, 0:256],
                                         AF.Exp, scale=-0.5),
           inc=s_lna, drain=True)
        rv = s_lna.n
        for (fc, s1v, xsl) in chunks:
            op("dve", lambda e, xsl=xsl: e.tensor_sub(sq_bf[:, 256:512], xsl,
                                                      mu_bc[:, 0:256]),
               waits=[(s_lna, rv)], drain=True)
            op("dve", lambda e, xsl=xsl: e.tensor_mul(xsl, sq_bf[:, 256:512],
                                                      r_bf[:, 0:256]),
               inc=s_s1, drain=True)

    project(0)
    drain_projq()
    for st in range(1, NST + 1):
        at = st - 1
        attend_tile(4 * at + 0, at)
        attend_tile(4 * at + 1, at)
        att_flush()
        wfc_ln(at, 0)
        if st < NST:
            project(st)
            drain_projq()
        attend_tile(4 * at + 2, at)
        attend_tile(4 * at + 3, at)
        att_flush()
        wfc_ln(at, 1)
    X1_DONE = s_s1.n

    RT_REF = [0]

    def router_batch(t8):
        w0 = claim(["r0"]) + [(s_s1, X1_DONE), (d_w, W_BASE)]
        im = 0
        for t in range(4):
            tt = 4 * t8 + t
            for i in range(8):
                def mk(e, i=i, tt=tt, t=t):
                    return e.matmul(
                        P01[:, t * 16:t * 16 + 16],
                        x_bf[:, i * S + tt * 128:i * S + tt * 128 + 128],
                        rw_sb[:, i * 16:i * 16 + 16],
                        start=(i == 0), stop=(i == 7))
                im += 1
                op("te", mk, waits=(w0 if im == 1 else ()),
                   inc=(s_rt if im == 32 else None))
        rv = s_rt.n
        op("dve", lambda e: e.tensor_copy(lg_sb[:, 0:64], P01[:, 0:64]),
           waits=[(s_rt, rv)], inc=s_rtd, drain=True)
        release(["r0"], s_rtd)

        def T4(fn, eng="dve", waits=(), inc=None):
            for t in range(4):
                op(eng, (lambda e, t=t: fn(e, t)),
                   waits=(waits if t == 0 else ()),
                   inc=(inc if t == 3 else None), drain=(t == 0))
        T4(lambda e, t: e.reduce_max(t2a[:, 8 * t:8 * t + 1],
                                     lg_sb[:, 16 * t:16 * t + 16], axis=AX.X))
        T4(lambda e, t: e.tensor_single_scalar(
               t2b[:, 16 * t:16 * t + 16], lg_sb[:, 16 * t:16 * t + 16],
               t2a[:, 8 * t:8 * t + 1], op=ALU.is_ge))
        T4(lambda e, t: e.scalar_tensor_tensor(
               lg_sb[:, 16 * t:16 * t + 16], t2b[:, 16 * t:16 * t + 16],
               -1e30, lg_sb[:, 16 * t:16 * t + 16], op0=ALU.mult,
               op1=ALU.add))
        T4(lambda e, t: e.reduce_max(t2a[:, 8 * t + 1:8 * t + 2],
                                     lg_sb[:, 16 * t:16 * t + 16], axis=AX.X))
        T4(lambda e, t: e.tensor_sub(t2a[:, 8 * t + 2:8 * t + 3],
                                     t2a[:, 8 * t:8 * t + 1],
                                     t2a[:, 8 * t + 1:8 * t + 2]), inc=s_rtd)
        dv = s_rtd.n
        T4(lambda e, t: e.activation(t2a[:, 8 * t + 3:8 * t + 4],
                                     t2a[:, 8 * t + 2:8 * t + 3], AF.Sigmoid),
           eng="act", waits=[(s_rtd, dv)], inc=s_rta)
        av2 = s_rta.n
        T4(lambda e, t: e.tensor_scalar(t2a[:, 8 * t + 4:8 * t + 5],
                                        t2a[:, 8 * t + 3:8 * t + 4], -1.0,
                                        1.0, op0=ALU.mult, op1=ALU.add),
           waits=[(s_rta, av2)])
        T4(lambda e, t: e.tensor_single_scalar(
               lg_sb[:, 16 * t:16 * t + 16], lg_sb[:, 16 * t:16 * t + 16],
               t2a[:, 8 * t + 1:8 * t + 2], op=ALU.is_ge))
        T4(lambda e, t: e.scalar_tensor_tensor(
               comb_sb[:, 16 * t:16 * t + 16], t2b[:, 16 * t:16 * t + 16],
               t2a[:, 8 * t + 3:8 * t + 4], t2b[:, 16 * t:16 * t + 16],
               op0=ALU.mult, op1=ALU.bypass))
        T4(lambda e, t: e.scalar_tensor_tensor(
               comb_bf[:, 16 * t:16 * t + 16], lg_sb[:, 16 * t:16 * t + 16],
               t2a[:, 8 * t + 4:8 * t + 5], comb_sb[:, 16 * t:16 * t + 16],
               op0=ALU.mult, op1=ALU.add), inc=s_rtd)
        dv2 = s_rtd.n
        tw = claim(["p4"]) + [(s_rtd, dv2), (s_ri, 2)]
        for t in range(4):
            op("te", lambda e, t=t: e.transpose(
                   P4[0:16, t * 128:t * 128 + 128],
                   comb_bf[:, 16 * t:16 * t + 16], ident[:, :]),
               waits=(tw if t == 0 else ()), inc=(s_rt if t == 3 else None))
        trv = s_rt.n
        op("act", lambda e, t8=t8: e.copy(
               combT[0:16, t8 * 512:t8 * 512 + 512], P4[0:16, 0:512]),
           waits=[(s_rt, trv)], inc=s_rta)
        release(["p4"], s_rta)
        RT_REF[0] = s_rta.n

    slot_rel = {}
    ew_ready = {}

    def load_expert(i):
        sl = eslot[i % NSLOT]
        dsem = d_es[i % NSLOT]
        w = [(s_s1, X1_DONE)]
        if i % NSLOT in slot_rel:
            w.append((s_grp, slot_rel[i % NSLOT]))
        if i < 4:
            n, hf = i // 2, i % 2
            srcs = (wg_s[n, :, hf * 256:(hf + 1) * 256],
                    wu_s[n, :, hf * 256:(hf + 1) * 256],
                    wd_s[n, hf * 256:(hf + 1) * 256, :])
        else:
            ei = i - 4
            srcs = (wg_r[ei], wu_r[ei], wd_r[ei])
        dma("pool", sl[:, 0:2048].rearrange("p (kc m) -> p kc m", kc=8),
            srcs[0].rearrange("(kc p) m -> p kc m", p=128), dsem, waits=w)
        dma("pool", sl[:, 2048:4096].rearrange("p (kc m) -> p kc m", kc=8),
            srcs[1].rearrange("(kc p) m -> p kc m", p=128), dsem)
        dma("pool", sl[:, 4096:6144].rearrange("p (mc f) -> p mc f", mc=2),
            srcs[2].rearrange("(mc p) f -> p mc f", p=128), dsem)
        ew_ready[i] = (dsem, dsem.n)

    cr_ready = {}
    cb_gate = [0, 0]
    cb_base = [0]

    def ffn_group(gi):
        grp = GROUPS[gi]
        first, last = (gi == 0), (gi == len(GROUPS) - 1)
        routed_g = [ei for ei in grp if ei >= 4]
        cb_base[0] = s_cb.n
        for t8 in range(NFT):
            tok0 = t8 * 512
            if first:
                router_batch(t8)
            if last:
                tl_list = ([0, 1] if NFT > 1 else [0]) if t8 == 0 else (
                    [t8 + 1] if t8 + 1 < NFT else [])
                for tl in tl_list:
                    fas = d_fa2[tl % 2]
                    w = [(d_acc2[0], d_acc2[0].n), (d_acc2[1], d_acc2[1].n)]
                    if tl >= 2:
                        w.append((s_s2, 8 * (tl - 1)))
                    for fc in range(8):
                        dma("sync", acc_ld[:, (tl % 2) * 4096 + fc * 512:
                                           (tl % 2) * 4096 + fc * 512 + 512],
                            facc[fc, :, tl * 512:tl * 512 + 512], fas,
                            waits=w)
                    acc_ready[tl] = (fas, fas.n)
            if routed_g:
                nxt = t8 + 1 if t8 + 1 < NFT else None
                tgts = [t8] if t8 == 0 else ([] if nxt is None else [nxt])
                if t8 == 0 and NFT > 1:
                    tgts = [0, 1]
                for tl in tgts:
                    crs = d_cr2[tl % 2]
                    cw = [(s_rta, RT_REF[0]), (s_cb, cb_gate[tl % 2])]
                    cb_gate[tl % 2] = cb_base[0] + len(routed_g) * (tl + 1)
                    for j, ei in enumerate(grp):
                        if ei < 4:
                            continue
                        dma("sync",
                            comb_rowT[0:1, (tl % 2) * 2048 + j * 512:
                                      (tl % 2) * 2048 + j * 512 + 512],
                            combT[ei - 4:ei - 3, tl * 512:tl * 512 + 512],
                            crs, waits=cw)
                    for ei in routed_g:
                        cr_ready[(ei, tl)] = (crs, crs.n)

            for j, ei in enumerate(grp):
                sl = eslot[ei % NSLOT]
                routed = ei >= 4
                cbs = comb_bc[:, (j % 2) * 512:(j % 2) * 512 + 512]
                if routed:
                    cw = claim(["p7"]) + [cr_ready[(ei, t8)],
                                          (s_cbe, max(0, s_cbe.n - 1))]
                    op("te", lambda e, t8=t8, j=j: e.matmul(
                           P7[:, 0:512], ones1r[:, :],
                           comb_rowT[0:1, (t8 % 2) * 2048 + j * 512:
                                     (t8 % 2) * 2048 + j * 512 + 512],
                           start=True, stop=True),
                       waits=cw, inc=s_cb)
                    cbv = s_cb.n
                    op("act", lambda e, cbs=cbs: e.copy(cbs, P7[:, 0:512]),
                       waits=[(s_cb, cbv)], inc=s_cbe)
                    cbev = s_cbe.n
                    release(["p7"], s_cbe)
                for (dst, woff, smm, res) in ((P01, 0, s_g, ["r0", "r1"]),
                                              (P23, 2048, s_u, ["r2", "r3"])):
                    w0 = claim(res) + [ew_ready[ei],
                                       (s_s1, X1_DONE)]
                    for mc in range(2):
                        for i in range(8):
                            def mk(e, i=i, mc=mc, dst=dst, woff=woff, sl=sl,
                                   tok0=tok0):
                                return e.matmul(
                                    dst[:, mc * 512:mc * 512 + 512],
                                    sl[:, woff + i * 256 + mc * 128:
                                       woff + i * 256 + mc * 128 + 128],
                                    x_bf[:, i * S + tok0:i * S + tok0 + 512],
                                    start=(i == 0), stop=(i == 7))
                            op("te", mk,
                               waits=(w0 if (mc == 0 and i == 0) else ()),
                               inc=(smm if (mc == 1 and i == 7) else None))
                gv, uv = s_g.n, s_u.n
                op("act", lambda e: e.activation(silu_sb[:, :], P01[:, :],
                                                 AF.Silu),
                   waits=[(s_g, gv), (s_h, s_h.n)], inc=s_sil)
                sv_ = s_sil.n
                release(["r0", "r1"], s_sil)
                hsl = h_sb[:, j * 1024:j * 1024 + 1024]
                hw2 = [(s_u, uv), (s_dn, s_dn.n)]
                if routed:
                    for mc in range(2):
                        op("dve", lambda e, mc=mc, hsl=hsl, cbs=cbs:
                               e.scalar_tensor_tensor(
                                   hsl[:, mc * 512:mc * 512 + 512],
                                   P23[:, mc * 512:mc * 512 + 512], 1.0, cbs,
                                   op0=ALU.mult, op1=ALU.mult),
                           waits=(hw2 + [(s_cbe, cbev)] if mc == 0 else ()),
                           inc=(s_usc if mc == 1 else None), drain=(mc == 0))
                else:
                    op("dve", lambda e, hsl=hsl: e.tensor_copy(hsl, P23[:, :]),
                       waits=hw2, inc=s_usc, drain=True)
                uscv = s_usc.n
                release(["r2", "r3"], s_usc)
                op("dve", lambda e, hsl=hsl: e.tensor_mul(hsl, hsl,
                                                          silu_sb[:, :]),
                   waits=[(s_usc, uscv), (s_sil, sv_)], inc=s_h, drain=True)
            hv_all = s_h.n
            bank_ap = {"p5": P5[:, 0:512], "p6": P6[:, 0:512],
                       "p7": P7[:, 0:512],
                       "r0": P01[:, 0:512], "r1": P01[:, 512:1024],
                       "r2": P23[:, 0:512], "r3": P23[:, 512:1024]}
            for fcs, bres in (((0, 1, 2), ("p5", "p6", "p7")),
                              ((3, 4, 5), ("p5", "p6", "p7")),
                              ((6, 7), ("p5", "p6"))):
                w_bank = {fi: claim([bres[fi]]) for fi in range(len(fcs))}
                w_bank[0] = w_bank[0] + [(s_h, hv_all)]
                stop_v = {}
                for j, ei in enumerate(grp):
                    sl = eslot[ei % NSLOT]
                    for fi, fc in enumerate(fcs):
                        ps_ap = bank_ap[bres[fi]]
                        for mc in range(2):
                            def mk(e, sl=sl, fc=fc, mc=mc, ps_ap=ps_ap, j=j):
                                return e.matmul(
                                    ps_ap,
                                    sl[:, 4096 + mc * 1024 + fc * 128:
                                       4096 + mc * 1024 + fc * 128 + 128],
                                    h_sb[:, j * 1024 + mc * 512:
                                         j * 1024 + mc * 512 + 512],
                                    start=(j == 0 and mc == 0),
                                    stop=(j == len(grp) - 1 and mc == 1))
                            is_stop = (j == len(grp) - 1 and mc == 1)
                            op("te", mk,
                               waits=(w_bank[fi] if (j == 0 and mc == 0)
                                      else ()),
                               inc=(s_dn if is_stop else None))
                            if is_stop:
                                stop_v[fi] = s_dn.n
                for fi, fc in enumerate(fcs):
                    ps = bank_ap[bres[fi]]
                    dnv = stop_v[fi]
                    if last:
                        asl = acc_ld[:, (t8 % 2) * 4096 + fc * 512:
                                     (t8 % 2) * 4096 + fc * 512 + 512]
                        xsl = x_bf[:, fc * S + tok0:fc * S + tok0 + 512]
                        op("dve", lambda e, ps=ps, asl=asl, xsl=xsl:
                               e.scalar_tensor_tensor(xsl, ps, 1.0, asl,
                                                      op0=ALU.mult,
                                                      op1=ALU.add),
                           waits=[(s_dn, dnv),
                                  acc_ready[t8]],
                           inc=s_s2, drain=True)
                        release([bres[fi]], s_s2)
                    else:
                        sgl = stage[:, (fc % 2) * 512:(fc % 2) * 512 + 512]
                        accs = d_acc2[fc % 2]
                        dw2 = [(s_dn, dnv), (accs, accs.n)]
                        if first:
                            xsl = x_bf[:, fc * S + tok0:fc * S + tok0 + 512]
                            op("dve", lambda e, ps=ps, sgl=sgl, xsl=xsl:
                                   e.scalar_tensor_tensor(sgl, ps, 1.0, xsl,
                                                          op0=ALU.mult,
                                                          op1=ALU.add),
                               waits=dw2, inc=s_dne, drain=True)
                        else:
                            op("dve", lambda e, ps=ps, sgl=sgl: e.tensor_copy(
                                   sgl, ps),
                               waits=dw2, inc=s_dne, drain=True)
                        ev = s_dne.n
                        release([bres[fi]], s_dne)
                        dma("pool", facc[fc, :, tok0:tok0 + 512], sgl, accs,
                            waits=[(s_dne, ev)], accum=(not first))
            if last:
                s2v = s_s2.n
                sw = claim(["p5", "p6"])
                st_base = s_stat.n
                for j in range(8):
                    fc = j
                    xsl = x_bf[:, fc * S + tok0:fc * S + tok0 + 512]
                    op("te", lambda e, xsl=xsl, j=j: e.matmul(
                           P5[:, 0:512], ones128[:, :], xsl, start=(j == 0),
                           stop=(j == 7)),
                       waits=([(s_s2, s2v)] + sw if j == 0 else
                              [(s_s2, s2v)]),
                       inc=s_stat)
                sq_v = []
                for j in range(8):
                    fc = j
                    xsl = x_bf[:, fc * S + tok0:fc * S + tok0 + 512]
                    sqs = sq_bf[:, (j % 2) * 512:(j % 2) * 512 + 512]
                    w = [(s_s2, s2v)]
                    if j >= 2:
                        w.append((s_stat, st_base + 8 + (j - 2) + 1))
                    op("act", lambda e, xsl=xsl, sqs=sqs: e.activation(
                           sqs, xsl, AF.Square),
                       waits=w, inc=s_lna, drain=True)
                    sq_v.append(s_lna.n)
                for j in range(8):
                    sqs = sq_bf[:, (j % 2) * 512:(j % 2) * 512 + 512]
                    op("te", lambda e, j=j, sqs=sqs: e.matmul(
                           P6[:, 0:512], ones128[:, :], sqs, start=(j == 0),
                           stop=(j == 7)),
                       waits=[(s_lna, sq_v[j])], inc=s_stat)
                stv = s_stat.n
                op("dve", lambda e: e.tensor_single_scalar(
                       mu_bc[:, :], P5[:, :], 1.0 / D, op=ALU.mult),
                   waits=[(s_stat, stv)], drain=True)
                op("dve", lambda e: e.tensor_mul(var_t[:, :], mu_bc[:, :],
                                                 mu_bc[:, :]), drain=True)
                op("dve", lambda e: e.scalar_tensor_tensor(
                       var_t[:, :], P6[:, :], 1.0 / D, var_t[:, :],
                       op0=ALU.mult, op1=ALU.subtract), inc=s_lnd, drain=True)
                lv = s_lnd.n
                release(["p5", "p6"], s_lnd)
                op("act", lambda e: e.activation(var_t[:, :], var_t[:, :],
                                                 AF.Ln, bias=eps_sb[:, 0:1]),
                   waits=[(s_lnd, lv)], drain=True)
                op("act", lambda e: e.activation(r_bf[:, :], var_t[:, :],
                                                 AF.Exp, scale=-0.5),
                   inc=s_lna, drain=True)
                rv = s_lna.n
                for fc in range(8):
                    xsl = x_bf[:, fc * S + tok0:fc * S + tok0 + 512]
                    osl = out_st2[:, (fc % 2) * 512:(fc % 2) * 512 + 512]
                    ods = d_out2[fc % 2]
                    op("dve", lambda e, xsl=xsl: e.tensor_sub(
                           sq_bf[:, 0:512], xsl, mu_bc[:, :]),
                       waits=[(s_lna, rv), (ods, ods.n)],
                       drain=True)
                    op("dve", lambda e, osl=osl: e.tensor_mul(
                           osl, sq_bf[:, 0:512], r_bf[:, :]),
                       inc=s_x2, drain=True)
                    xv = s_x2.n
                    dma("sync", outT[fc * 128:fc * 128 + 128, tok0:tok0 + 512],
                        osl, ods, waits=[(s_x2, xv)])
        op("te", lambda e: e.nop(), inc=s_grp)
        for ei in grp:
            slot_rel[ei % NSLOT] = s_grp.n

    for i in GROUPS[0]:
        load_expert(i)
    acc_ready = {}
    for gi in range(len(GROUPS)):
        ffn_group(gi)
        if gi + 1 < len(GROUPS):
            for i in GROUPS[gi + 1]:
                load_expert(i)

    P["sync"].append(lambda e: e.wait_ge(d_out2[0].h, d_out2[0].n))
    P["sync"].append(lambda e: e.wait_ge(d_out2[1].h, d_out2[1].n))

    bcm = nc.Block()
    block = bcm.__enter__()

    def runner(eng):
        def f(e):
            for g in P[eng]:
                g(e)
        return f
    block.tensor(runner("te"))
    block.vector(runner("dve"))
    block.scalar(runner("act"))
    block.gpsimd(runner("pool"))
    block.sync(runner("sync"))
    bcm.__exit__(None, None, None)
    return nc


def kernel(L0_attn_x, L0_attn_alpha, L0_attn_Wq, L0_attn_bq, L0_attn_Wk,
           L0_attn_bk, L0_attn_Wv, L0_attn_bv, L0_attn_W_o, L0_attn_Wfc,
           L0_attn_bfc, L0_ffn_router_w, L0_ffn_wg_r, L0_ffn_wu_r,
           L0_ffn_wd_r, L0_ffn_wg_s, L0_ffn_wu_s, L0_ffn_wd_s):
    x = np.asarray(L0_attn_x, dtype=np.float32)
    B, S, _ = x.shape
    f32 = lambda a: np.ascontiguousarray(np.asarray(a, dtype=np.float32))
    xl = f32(x[:, -2:, :].reshape(B, 2 * D))
    alpha = f32(L0_attn_alpha).reshape(1, 1)
    b2 = lambda b: f32(np.asarray(b, dtype=np.float32).reshape(8, 128).T)
    woa = f32(L0_attn_W_o)
    wo2 = np.zeros((128, 1024), np.float32)
    for pp in range(8):
        wo2[0:64, pp * 128:pp * 128 + 64] = woa[2 * pp]
        wo2[64:128, pp * 128 + 64:pp * 128 + 128] = woa[2 * pp + 1]
    wo2 = f32(wo2)
    common = {"xl": xl, "alpha": alpha, "wq": f32(L0_attn_Wq),
              "wk": f32(L0_attn_Wk), "wv": f32(L0_attn_Wv),
              "wfc": f32(L0_attn_Wfc), "bq": b2(L0_attn_bq),
              "bk": b2(L0_attn_bk), "bfc": b2(L0_attn_bfc),
              "bv": b2(L0_attn_bv), "wo2": wo2, "rw": f32(L0_ffn_router_w),
              "wg_r": f32(L0_ffn_wg_r), "wu_r": f32(L0_ffn_wu_r),
              "wd_r": f32(L0_ffn_wd_r), "wg_s": f32(L0_ffn_wg_s),
              "wu_s": f32(L0_ffn_wu_s), "wd_s": f32(L0_ffn_wd_s)}
    nc = build_nc(S)
    in_maps = [dict(common, xT=np.ascontiguousarray(x[c].T)) for c in range(B)]
    res = run_bass_kernel_spmd(nc, in_maps, core_ids=list(range(B)))
    out = np.stack([np.asarray(r["outT"]).T for r in res.results], axis=0)
    return out.astype(np.float32)


# version 17
# speedup vs baseline: 1.1490x; 1.1490x over previous
import numpy as np
import concourse.bass as bass
import concourse.mybir as mybir
from concourse.bass_utils import run_bass_kernel_spmd

F32 = mybir.dt.float32
BF16 = mybir.dt.bfloat16
AF = mybir.ActivationFunctionType
ALU = mybir.AluOpType
AX = mybir.AxisListType

D = 1024
RING = 8
DVE_DRAIN = True
DT_B = {F32: 4, BF16: 2}
GROUPS = [list(range(4 * g, 4 * g + 4)) for g in range(5)]
NSLOT = 6


class Sem:
    def __init__(self, nc, name):
        self.h = nc.alloc_semaphore(name)
        self.n = 0


def build_nc(S=4096):
    NT = S // 128
    NST = S // 512
    NFT = S // 512

    nc = bass.Bass("TRN2", target_bir_lowering=False)

    din = lambda n, s: nc.dram_tensor(n, list(s), F32, kind="ExternalInput").ap()
    xT = din("xT", [D, S])
    xl_d = din("xl", [8, 2 * D])
    alpha_d = din("alpha", [1, 1])
    wq_d, wk_d, wv_d, wfc_d = (din(n, [D, D]) for n in ("wq", "wk", "wv", "wfc"))
    bq_d, bk_d, bfc_d, bv_d = (din(n, [128, 8]) for n in ("bq", "bk", "bfc", "bv"))
    wo_d = din("wo2", [128, 1024])
    rw_d = din("rw", [D, 16])
    wg_r = din("wg_r", [16, D, 256])
    wu_r = din("wu_r", [16, D, 256])
    wd_r = din("wd_r", [16, 256, D])
    wg_s = din("wg_s", [2, D, 512])
    wu_s = din("wu_s", [2, D, 512])
    wd_s = din("wd_s", [2, 512, D])
    outT = nc.dram_tensor("outT", [D, S], F32, kind="ExternalOutput").ap()
    facc = nc.dram_tensor("facc", [8, 128, S], BF16, kind="Internal").ap()

    SB_BASE = 17408
    cur = [SB_BASE]

    def A(name, shape, dtype):
        nb = int(np.prod(shape[1:])) * DT_B[dtype]
        off = (cur[0] + 63) // 64 * 64
        t = nc.alloc_sbuf_tensor_at(name, list(shape), dtype, offset=off)
        cur[0] = off + nb
        return t

    x_bf = A("x_bf", [128, 8 * S], BF16)
    mask_mid = A("mask_mid", [128, 768], BF16)
    ones128 = A("ones128", [128, 128], BF16)
    ident = A("ident", [128, 128], BF16)
    ones1r = A("ones1r", [1, 128], BF16)
    ones8f = A("ones8f", [8, 128], F32)
    ones1f = A("ones1f", [1, 128], F32)
    bq8 = A("bq8", [128, 8], F32)
    bk_s = A("bk_s", [128, 8], F32)
    bfc_s = A("bfc_s", [128, 8], F32)
    bv_s = A("bv_s", [128, 8], F32)
    hh_sb = A("hh_sb", [128, 2], F32)
    sct = A("sct", [128, 8], F32)
    alpha_s = A("alpha_s", [1, 1], F32)
    eps_sb = A("eps_sb", [128, 1], F32)
    negmax = A("negmax", [128, 8], F32)
    den = A("den", [128, 16], F32)
    rden = A("rden", [128, 16], F32)
    mu_bc = A("mu_bc", [128, 512], BF16)
    var_t = A("var_t", [128, 512], F32)
    r_bf = A("r_bf", [128, 512], BF16)
    sq_bf = A("sq_bf", [128, 1024], BF16)
    rw_sb = A("rw_sb", [128, 8 * 16], BF16)
    PERS_END = cur[0]

    OV = (PERS_END + 63) // 64 * 64
    cur[0] = OV
    wq_sb = A("wq_sb", [128, 8 * 1024], BF16)
    wk_sb = A("wk_sb", [128, 8 * 1024], BF16)
    wv_sb = A("wv_sb", [128, 8 * 1024], BF16)
    wfc_sb = A("wfc_sb", [128, 8 * 1024], BF16)
    wo_sb = A("wo_sb", [128, 1024], BF16)
    k_ring = A("k_ring", [128, 8 * RING * 128], BF16)
    v_ring = A("v_ring", [128, RING * 1024], BF16)
    q_bf = A("q_bf", [128, 2 * 8 * 512], BF16)
    E_sb = A("E_sb", [128, 3 * 768], BF16)
    AT_sb = A("AT_sb", [128, 3 * 768], BF16)
    ao_bf = A("ao_bf", [128, 2 * 128], BF16)
    out2_bf = A("out2_bf", [128, 8 * 256], BF16)
    ATTN_END = cur[0]

    cur[0] = OV
    eslot = [A(f"eslot{i}", [128, 6144], BF16) for i in range(NSLOT)]
    combT = A("combT", [16, S], BF16)
    comb_rowT = A("comb_rowT", [1, 2 * 4 * 512], BF16)
    comb_bc = A("comb_bc", [128, 2 * 512], BF16)
    silu_sb = A("silu_sb", [128, 1024], BF16)
    h_sb = A("h_sb", [128, 4 * 1024], BF16)
    stage = A("stage", [128, 2 * 512], BF16)
    acc_ld = A("acc_ld", [128, 2 * 8 * 512], BF16)
    lg_sb = A("lg_sb", [128, 64], F32)
    t2a = A("t2a", [128, 32], F32)
    t2b = A("t2b", [128, 64], F32)
    comb_sb = A("comb_sb", [128, 64], F32)
    comb_bf = A("comb_bf", [128, 64], BF16)
    out_st2 = A("out_st2", [128, 2 * 512], F32)
    FFN_END = cur[0]

    LIMIT = 229216
    assert max(ATTN_END, FFN_END) <= LIMIT, (PERS_END, ATTN_END, FFN_END)

    klv = k_ring[:, :].bitcast(F32)
    xl_t = klv[0:8, 0:2 * D]
    d8_t = klv[0:8, 2 * D:3 * D]
    ds_t = klv[0:8, 3 * D:3 * D + 1]
    assert 3 * D + 1 <= RING * 8 * 128 // 2
    DIFF = AT_sb[:, :].bitcast(F32)[:, 0:768]

    P01 = nc.alloc_psum_tensor("P01", [128, 1024], F32)
    P23 = nc.alloc_psum_tensor("P23", [128, 1024], F32)
    P4 = nc.alloc_psum_tensor("P4", [128, 1024], BF16)
    P5 = nc.alloc_psum_tensor("P5", [128, 512], F32)
    P6 = nc.alloc_psum_tensor("P6", [128, 512], F32)
    P7 = nc.alloc_psum_tensor("P7", [128, 512], F32)

    ENGS = ("te", "dve", "act", "pool", "sync")
    P = {e: [] for e in ENGS}
    seen = {e: {} for e in ENGS}

    def sems(*names):
        return [Sem(nc, n) for n in names]

    (d_x, d_w, d_ld, d_out2, d_acc2, d_cr2, d_fa2) = (
        sems("d_x", "d_w", "d_ld")
        + [sems("d_outa", "d_outb"), sems("d_aca", "d_acb"),
           sems("d_cra", "d_crb"), sems("d_faa", "d_fab")])
    d_es = sems(*[f"d_es{i}" for i in range(NSLOT)])
    (s_pj, s_pjev, s_sc, s_mx, s_ex, s_A, s_tr, s_atev, s_av, s_aoev, s_wo,
     s_o2ev, s_wfc, s_s1, s_stat, s_lnd, s_lna, s_hh, s_msk, s_ri) = sems(
        "s_pj", "s_pjev", "s_sc", "s_mx", "s_ex", "s_A", "s_tr", "s_atev",
        "s_av", "s_aoev", "s_wo", "s_o2ev", "s_wfc", "s_s1", "s_stat",
        "s_lnd", "s_lna", "s_hh", "s_msk", "s_ri")
    (s_dva, s_dvb) = sems("s_dva", "s_dvb")
    (s_rt, s_rtd, s_rta, s_g, s_u, s_cb, s_cbe, s_sil, s_usc, s_h, s_dn,
     s_dne, s_grp, s_s2, s_x2) = sems(
        "s_rt", "s_rtd", "s_rta", "s_g", "s_u", "s_cb", "s_cbe", "s_sil",
        "s_usc", "s_h", "s_dn", "s_dne", "s_grp", "s_s2", "s_x2")

    def op(eng, fn, inc=None, inc_n=1, waits=(), drain=False):
        w = []
        for (s, v) in waits:
            if v > 0 and seen[eng].get(s, -1) < v:
                seen[eng][s] = v
                w.append((s.h, v))
        if drain and (eng != "dve" or DVE_DRAIN):
            P[eng].append(lambda e: e.drain())

        def g(e, fn=fn, w=tuple(w), inc=inc, inc_n=inc_n):
            for (h, v) in w:
                e.wait_ge(h, v)
            r = fn(e)
            if inc is not None:
                r.then_inc(inc.h, inc_n)
        P[eng].append(g)
        if inc is not None:
            inc.n += inc_n
        return inc.n if inc is not None else None

    def dma(eng, out, in_, inc, waits=(), accum=False, maxdim=None):
        def f(e, out=out, in_=in_, accum=accum, maxdim=maxdim):
            kw = {}
            if maxdim is not None:
                kw["max_dma_last_dim"] = maxdim
            if accum:
                return e.dma_start(out=out, in_=in_, accum_op=ALU.add, **kw)
            return e.dma_start(out=out, in_=in_, **kw)
        return op(eng, f, inc=inc, inc_n=16, waits=waits)

    rel = {}

    def claim(res_list):
        w = []
        for r in res_list:
            if r in rel:
                w.append(rel[r])
        return w

    def release(res_list, sem):
        for r in res_list:
            rel[r] = (sem, sem.n)
    ROT = [(P01, 0), (P01, 512), (P23, 0), (P23, 512)]
    rot_i = [0]

    def dma_x(sx, w_):
        dma("pool",
            x_bf[:, :].rearrange("p (c t) -> p c t", c=8)[:, :, sx * w_:
                                                          (sx + 1) * w_],
            xT.rearrange("(c p) t -> p c t", p=128)[:, :, sx * w_:
                                                    (sx + 1) * w_], d_x)

    def dma_w(dst, src):
        dma("pool", dst[:, :].rearrange("p (c f) -> p c f", c=8),
            src.rearrange("(c p) f -> p c f", p=128), d_w)

    XW = S // 4
    dma_w(wq_sb, wq_d)
    WQ_N = d_w.n
    dma_x(0, XW)
    dma_w(wk_sb, wk_d)
    WK_N = d_w.n
    dma_w(wv_sb, wv_d)
    WV_N = d_w.n
    dma_x(1, XW)
    dma_x(2, XW)
    dma_w(wfc_sb, wfc_d)
    WFC_N = d_w.n
    dma_x(3, XW)
    D_X_ALL = d_x.n
    X_N = [16, 32, 48, 64]
    dma("pool", wo_sb[:, :], wo_d, d_w)
    WO_N = d_w.n
    dma("pool", rw_sb[:, :].rearrange("p (c f) -> p c f", c=8),
        rw_d.rearrange("(c p) f -> p c f", p=128), d_w)
    W_BASE = d_w.n

    for (dst, src) in ((bq8, bq_d), (bk_s, bk_d), (bfc_s, bfc_d), (bv_s, bv_d),
                       (alpha_s, alpha_d)):
        dma("sync", dst[:, :], src, d_ld)
    dma("sync", xl_t, xl_d, d_ld)
    LD_ALL = d_ld.n

    op("dve", lambda e: e.memset(ones128[:, :], 1.0))
    op("dve", lambda e: e.memset(ones1r[:, :], 1.0))
    op("dve", lambda e: e.memset(ones8f[:, :], 1.0))
    op("dve", lambda e: e.memset(ones1f[:, :], 1.0))
    op("dve", lambda e: e.memset(eps_sb[:, :], 1e-5))
    op("dve", lambda e: e.memset(v_ring[:, :], 0.0), inc=s_ri)
    op("pool", lambda e: e.iota(ident[:, :], pattern=[[-1, 128]], base=0,
                                channel_multiplier=1,
                                allow_small_or_imprecise_dtypes=True),
       inc=s_msk)
    op("pool", lambda e: e.iota(DIFF, pattern=[[0, 2], [-1, 384]], base=128,
                                channel_multiplier=1,
                                allow_small_or_imprecise_dtypes=True),
       inc=s_msk)
    op("dve", lambda e: e.tensor_single_scalar(ident[:, :], ident[:, :], 0.0,
                                               op=ALU.is_equal),
       waits=[(s_msk, 1)], inc=s_ri, drain=True)

    op("dve", lambda e: e.tensor_sub(d8_t, xl_t[:, 0:D], xl_t[:, D:2 * D]),
       waits=[(d_ld, LD_ALL)], inc=s_hh, drain=True)
    op("act", lambda e: e.activation(d8_t, d8_t, AF.Abs),
       waits=[(s_hh, 1)], inc=s_hh)
    op("dve", lambda e: e.reduce_sum(ds_t, d8_t, axis=AX.X),
       waits=[(s_hh, 2)], inc=s_hh, drain=True)
    hw = claim(["p7"])
    op("te", lambda e: e.matmul(P7[:, 0:1], ones8f[:, :], ds_t, start=True,
                                stop=True),
       waits=[(s_hh, 3), (s_ri, 2)] + hw)
    op("te", lambda e: e.matmul(P7[:, 1:2], ones1f[:, :], d8_t[0:1, 0:1],
                                start=True, stop=True))
    op("te", lambda e: e.matmul(P7[:, 2:3], ones1f[:, :], alpha_s[0:1, 0:1],
                                start=True, stop=True),
       waits=[(d_ld, LD_ALL)], inc=s_hh)
    op("dve", lambda e: e.tensor_single_scalar(sct[:, 0:1], P7[:, 0:1],
                                               1.0 / (8 * D), op=ALU.mult),
       waits=[(s_hh, 4)], drain=True)
    op("act", lambda e: e.activation(sct[:, 1:2], P7[:, 2:3], AF.Sigmoid),
       waits=[(s_hh, 4)], inc=s_hh)
    op("dve", lambda e: e.tensor_sub(sct[:, 2:3], sct[:, 0:1], P7[:, 1:2]),
       drain=True)
    op("dve", lambda e: e.tensor_mul(sct[:, 2:3], sct[:, 1:2], sct[:, 2:3]),
       waits=[(s_hh, 5)], drain=True)
    op("dve", lambda e: e.tensor_add(sct[:, 2:3], sct[:, 2:3], P7[:, 1:2]),
       inc=s_hh, drain=True)
    release(["p7"], s_hh)
    op("act", lambda e: e.activation(sct[:, 3:4], sct[:, 2:3], AF.Sigmoid),
       waits=[(s_hh, 6)], inc=s_hh, drain=True)
    op("dve", lambda e: e.tensor_scalar(hh_sb[:, 0:1], sct[:, 3:4], 48.0, 12.0,
                                        op0=ALU.mult, op1=ALU.add),
       waits=[(s_hh, 7)], drain=True)
    op("dve", lambda e: e.tensor_single_scalar(hh_sb[:, 1:2], hh_sb[:, 0:1],
                                               -1.0, op=ALU.mult),
       inc=s_hh, drain=True)
    m1s = E_sb[:, 0:768]
    op("dve", lambda e: e.tensor_single_scalar(m1s, DIFF, hh_sb[:, 0:1],
                                               op=ALU.is_le),
       waits=[(s_hh, 8), (s_msk, 2)], drain=True)
    op("dve", lambda e: e.tensor_single_scalar(mask_mid[:, :], DIFF,
                                               hh_sb[:, 1:2], op=ALU.is_ge),
       drain=True)
    op("dve", lambda e: e.tensor_mul(mask_mid[:, :], mask_mid[:, :], m1s),
       drain=True)
    op("dve", lambda e: e.tensor_scalar(mask_mid[:, :], mask_mid[:, :], 100.0,
                                        -100.0, op0=ALU.mult, op1=ALU.add),
       inc=s_msk, drain=True)
    MSK_DONE = s_msk.n
    op("dve", lambda e: e.memset(k_ring[:, :], 0.0),
       waits=[(s_hh, 4)], inc=s_ri, drain=True)
    RI_DONE = s_ri.n
    op("dve", lambda e: e.tensor_single_scalar(bq8[:, :], bq8[:, :], 0.125,
                                               op=ALU.mult),
       waits=[(d_ld, LD_ALL)], inc=s_hh, drain=True)
    HH_DONE = s_hh.n

    PROJQ = []

    def proj_set(mk_mm, evacs, extra_waits=()):
        PROJQ.append((mk_mm, evacs, list(extra_waits)))

    def emit_proj_set():
        if not PROJQ:
            return
        mk_mm, evacs, extra = PROJQ.pop(0)
        ri = rot_i[0] % 4
        rot_i[0] += 1
        ps_t, ps_o = ROT[ri]
        ps = ps_t[:, ps_o:ps_o + 512]
        w = claim([f"r{ri}"]) + extra
        for i in range(8):
            op("te", (lambda e, i=i, ps=ps: mk_mm(e, i, ps)),
               waits=(w if i == 0 else ()), inc=(s_pj if i == 7 else None))
        mmv = s_pj.n
        for ev in evacs:
            op("act", (lambda e, ev=ev, ps=ps: ev(e, ps)),
               waits=[(s_pj, mmv)], inc=s_pjev)
        release([f"r{ri}"], s_pjev)

    def drain_projq():
        while PROJQ:
            emit_proj_set()

    q_ready = {}
    k_ready = {}

    def project(st):
        base_w = [(d_x, X_N[st // 2]), (s_ri, RI_DONE), (s_hh, HH_DONE)]
        for fc in range(8):
            def mk(e, i, ps, fc=fc):
                return e.matmul(ps, wq_sb[:, i * 1024 + fc * 128:
                                          i * 1024 + fc * 128 + 128],
                                x_bf[:, i * S + st * 512:i * S + st * 512 + 512],
                                start=(i == 0), stop=(i == 7))

            def ev(e, ps, fc=fc):
                return e.activation(
                    q_bf[:, (st % 2) * 4096 + fc * 512:
                         (st % 2) * 4096 + fc * 512 + 512],
                    ps, AF.Identity, bias=bq8[:, fc:fc + 1], scale=0.125)
            proj_set(mk, [ev], extra_waits=base_w + [(d_w, WQ_N)])
        for fc in range(8):
            def mk(e, i, ps, fc=fc):
                return e.matmul(ps, wk_sb[:, i * 1024 + fc * 128:
                                          i * 1024 + fc * 128 + 128],
                                x_bf[:, i * S + st * 512:i * S + st * 512 + 512],
                                start=(i == 0), stop=(i == 7))
            evs = []
            for tt in range(4):
                slot = (4 * st + tt) % RING

                def ev(e, ps, fc=fc, tt=tt, slot=slot):
                    return e.activation(
                        k_ring[:, fc * RING * 128 + slot * 128:
                               fc * RING * 128 + slot * 128 + 128],
                        ps[:, tt * 128:tt * 128 + 128], AF.Identity,
                        bias=bk_s[:, fc:fc + 1], scale=1.0)
                evs.append(ev)
            proj_set(mk, evs, extra_waits=base_w + [(d_w, WK_N)])
        for tt in range(4):
            T = 4 * st + tt
            slot = T % RING
            for fh in range(2):
                def mk(e, i, ps, T=T, fh=fh):
                    return e.matmul(ps, x_bf[:, i * S + T * 128:
                                             i * S + T * 128 + 128],
                                    wv_sb[:, i * 1024 + fh * 512:
                                          i * 1024 + fh * 512 + 512],
                                    start=(i == 0), stop=(i == 7))

                def ev(e, ps, slot=slot, fh=fh):
                    return e.copy(v_ring[:, slot * 1024 + fh * 512:
                                         slot * 1024 + fh * 512 + 512], ps)
                proj_set(mk, [ev], extra_waits=base_w + [(d_w, WV_N)])
        done = s_pjev.n + sum(len(ev) for (_, ev, _) in PROJQ)
        q_ready[st % 2] = done
        for tt in range(4):
            k_ready[4 * st + tt + 1] = done
        k_ready[0] = k_ready.get(0, done)

    SCRES = [["r0", "r1"], ["r2", "r3"]]
    SC = [P01, P23]

    ATTST = {}
    GP = [0]
    E_LAST = {}
    AT_LAST = {}

    def emit_round(new_s, l1, l2, l3):
        if new_s is not None:
            s = new_s
            p, scp, par, T = s["p"], s["scp"], s["par"], s["T"]
            w0 = claim(SCRES[s["g"] % 2]) + [
                (s_pjev, q_ready[par]), (s_pjev, k_ready[min(T + 1, NT)]),
                (s_msk, MSK_DONE)]
            runs = []
            c = 0
            while c < 3:
                slot = (T - 1 + c) % RING
                w_ = 1
                while c + w_ < 3 and (T - 1 + c + w_) % RING == slot + w_:
                    w_ += 1
                runs.append((c, slot, w_))
                c += w_
            nmm = 2 * len(runs)
            im = 0
            for hi in range(2):
                b = hi * 64
                for (c0, slot, w_) in runs:
                    def mk(e, c0=c0, b=b, slot=slot, w_=w_, p=p, scp=scp,
                           par=par, T=T):
                        return e.matmul(
                            scp[:, (b // 64) * 512 + c0 * 128:
                                (b // 64) * 512 + c0 * 128 + 128 * w_],
                            q_bf[b:b + 64, par * 4096 + p * 512 + (T % 4) * 128:
                                 par * 4096 + p * 512 + (T % 4) * 128 + 128],
                            k_ring[b:b + 64, p * RING * 128 + slot * 128:
                                   p * RING * 128 + slot * 128 + 128 * w_],
                            start=True, stop=True)
                    im += 1
                    op("te", mk, waits=(w0 if im == 1 else ()),
                       inc=(s_sc if im == nmm else None))
            s["scv"] = s_sc.n
            segs = scp[:, :].rearrange("q (h w) -> q h w", h=2)[:, :, 0:384]
            s["segs"] = segs
        if l1 is not None:
            s = l1
            op("dve", lambda e, s=s: e.reciprocal(
                   rden[:, 2 * s["p8"]:2 * s["p8"] + 2],
                   den[:, 2 * s["p8"]:2 * s["p8"] + 2]),
               waits=[(s_ex, s["exv"])], inc=s_dvb)
            s["dvb"] = s_dvb.n
            op("dve", lambda e, s=s: e.tensor_single_scalar(
                   s["E"][:, 0:384], s["E"][:, 0:384],
                   rden[:, 2 * s["p8"]:2 * s["p8"] + 1], op=ALU.mult),
               waits=[(s_dvb, s["dvb"])])
            op("dve", lambda e, s=s: e.tensor_single_scalar(
                   s["E"][:, 384:768], s["E"][:, 384:768],
                   rden[:, 2 * s["p8"] + 1:2 * s["p8"] + 2], op=ALU.mult),
               inc=s_A)
            s["avl"] = s_A.n
            tw = claim(["p4"]) + [(s_A, s["avl"]), (s_ri, 2)]
            for c in range(6):
                op("te", lambda e, s=s, c=c: e.transpose(
                       P4[:, c * 128:c * 128 + 128],
                       s["E"][:, c * 128:c * 128 + 128], ident[:, :]),
                   waits=(tw if c == 0 else ()),
                   inc=(s_tr if c == 5 else None))
            s["trv"] = s_tr.n
            E_LAST[s["g"] % 3] = s["trv"]
        if l3 is not None:
            s = l3
            o2w = [(s_wo, s["wov"]), (s_wfc, s_wfc.n)]
            op("act", lambda e, s=s: e.copy(
                   out2_bf[:, s["o2c"]:s["o2c"] + 128], P7[:, 0:128]),
               waits=o2w, inc=s_o2ev)
            release(["p7"], s_o2ev)
        if l1 is not None:
            s = l1
            atw = [(s_tr, s["trv"])]
            if s["g"] % 3 in AT_LAST:
                atw.append((s_av, AT_LAST[s["g"] % 3]))
            op("act", lambda e, s=s: e.copy(s["AT"], P4[:, 0:768]),
               waits=atw, inc=s_atev)
            s["atv"] = s_atev.n
            release(["p4"], s_atev)
            lo = (s["g"] % 2 == 0)
            aw = claim(["p5" if lo else "p6"]) + [(s_atev, s["atv"])]
            for c in range(3):
                slot = (s["T"] - 1 + c) % RING
                for hi in range(2):
                    def mk(e, c=c, hi=hi, slot=slot, s=s, lo=lo):
                        dst = (P5[hi * 64:hi * 64 + 64, 0:128] if lo
                               else P6[hi * 64:hi * 64 + 64, 128:256])
                        return e.matmul(
                            dst,
                            v_ring[:, slot * 1024 + (2 * s["p"] + hi) * 64:
                                   slot * 1024 + (2 * s["p"] + hi) * 64 + 64],
                            s["AT"][:, hi * 384 + c * 128:
                                    hi * 384 + c * 128 + 128],
                            start=(c == 0), stop=(c == 2),
                            tile_position=(0, hi * 64), skip_group_check=True)
                    op("te", mk, waits=(aw if (c == 0 and hi == 0) else ()),
                       inc=(s_av if (c == 2 and hi == 1) else None))
            s["avv"] = s_av.n
            AT_LAST[s["g"] % 3] = s["avv"]
        if l2 is not None:
            s = l2
            lo = (s["g"] % 2 == 0)
            src_ps = P5[:, 0:128] if lo else P6[:, 128:256]
            op("act", lambda e, s=s, src_ps=src_ps: e.activation(
                   s["ao"], src_ps, AF.Identity,
                   bias=bv_s[:, s["p8"]:s["p8"] + 1], scale=1.0),
               waits=[(s_av, s["avv"])], inc=s_aoev)
            s["aov"] = s_aoev.n
            release(["p5" if (s["g"] % 2 == 0) else "p6"], s_aoev)
            ww = claim(["p7"]) + [(s_aoev, s["aov"]), (d_w, WO_N)]
            op("te", lambda e, s=s: e.matmul(
                   P7[:, 0:128],
                   wo_sb[:, s["p"] * 128:s["p"] * 128 + 128],
                   s["ao"][:, :], start=True, stop=True),
               waits=ww, inc=s_wo)
            s["wov"] = s_wo.n
        if new_s is not None:
            s = new_s
            op("dve", lambda e, segs=s["segs"]:
                   e.scalar_tensor_tensor(segs, segs, 1.0, mask_mid[:, :],
                                          op0=ALU.mult, op1=ALU.add),
               waits=[(s_sc, s["scv"])], inc=s_dva)
            if s["edge"] is not None:
                blk = 0 if s["edge"] == 0 else 256
                op("dve", lambda e, segs=s["segs"], blk=blk: e.memset(
                       segs[:, :, blk:blk + 128], -100.0), inc=s_dva)
            s["dva"] = s_dva.n
            op("dve", lambda e, s=s: e.reduce_max(
                   negmax[:, s["p8"]:s["p8"] + 1], s["segs"], axis=AX.XY,
                   negate=True),
               waits=[(s_dva, s["dva"])], inc=s_mx)
            s["mxv"] = s_mx.n
            ew = []
            if s["g"] % 3 in E_LAST:
                ew = [(s_tr, E_LAST[s["g"] % 3])]
            op("act", lambda e, s=s: e.activation(
                   s["E"][:, 0:384], s["scp"][:, 0:384], AF.Exp,
                   bias=negmax[:, s["p8"]:s["p8"] + 1],
                   accum_out=den[:, 2 * s["p8"]:2 * s["p8"] + 1]),
               waits=[(s_mx, s["mxv"])] + ew)
            op("act", lambda e, s=s: e.activation(
                   s["E"][:, 384:768], s["scp"][:, 512:896], AF.Exp,
                   bias=negmax[:, s["p8"]:s["p8"] + 1],
                   accum_out=den[:, 2 * s["p8"] + 1:2 * s["p8"] + 2]),
               inc=s_ex)
            s["exv"] = s_ex.n
            release(SCRES[s["g"] % 2], s_ex)
    def att_step(new_s):
        g = GP[0]
        l1 = ATTST.get(g - 1)
        l2 = ATTST.get(g - 2)
        l3 = ATTST.get(g - 3)
        emit_round(new_s, l1, l2, l3)
        if new_s is not None:
            ATTST[g] = new_s
            GP[0] += 1

    def attend_tile(T, st):
        par = st % 2
        edge = 0 if T == 0 else (1 if T == NT - 1 else None)
        for p in range(8):
            g = GP[0]
            s = {"p": p, "p8": p, "T": T, "par": par, "g": g, "edge": edge,
                 "scp": SC[g % 2],
                 "E": E_sb[:, (g % 3) * 768:(g % 3) * 768 + 768],
                 "AT": AT_sb[:, (g % 3) * 768:(g % 3) * 768 + 768],
                 "ao": ao_bf[:, (g % 2) * 128:(g % 2) * 128 + 128],
                 "o2c": p * 256 + (T % 2) * 128}
            att_step(s)

    def att_flush():
        for _ in range(3):
            g = GP[0]
            emit_round(None, ATTST.get(g - 1), ATTST.get(g - 2),
                       ATTST.get(g - 3))
            GP[0] += 1

    def wfc_ln(st, half):
        tok0 = st * 512 + half * 256
        o2v = s_o2ev.n
        chunks = []
        for fc in range(8):
            ri = rot_i[0] % 4
            rot_i[0] += 1
            ps_t, ps_o = ROT[ri]
            ps = ps_t[:, ps_o:ps_o + 256]
            w0 = claim([f"r{ri}"]) + [(s_o2ev, o2v), (d_w, WFC_N)]
            for i in range(8):
                def mk(e, i=i, fc=fc, ps=ps):
                    return e.matmul(ps, wfc_sb[:, i * 1024 + fc * 128:
                                               i * 1024 + fc * 128 + 128],
                                    out2_bf[:, i * 256:i * 256 + 256],
                                    start=(i == 0), stop=(i == 7))
                op("te", mk, waits=(w0 if i == 0 else ()),
                   inc=(s_wfc if i == 7 else None))
            mmv = s_wfc.n
            xsl = x_bf[:, fc * S + tok0:fc * S + tok0 + 256]
            op("dve", lambda e, ps=ps, xsl=xsl, fc=fc: e.scalar_tensor_tensor(
                   xsl, ps, bfc_s[:, fc:fc + 1], xsl, op0=ALU.add, op1=ALU.add),
               waits=[(s_wfc, mmv)], inc=s_s1, drain=True)
            release([f"r{ri}"], s_s1)
            chunks.append((fc, s_s1.n, xsl))
        sw = claim(["p5", "p6"])
        st_base = s_stat.n
        for j, (fc, s1v, xsl) in enumerate(chunks):
            op("te", lambda e, xsl=xsl, j=j: e.matmul(
                   P5[:, 0:256], ones128[:, :], xsl, start=(j == 0),
                   stop=(j == 7)),
               waits=([(s_s1, s1v)] + sw if j == 0 else [(s_s1, s1v)]),
               inc=s_stat)
        sq_v = []
        for j, (fc, s1v, xsl) in enumerate(chunks):
            sqs = sq_bf[:, (j % 4) * 256:(j % 4) * 256 + 256]
            w = [(s_s1, s1v)]
            if j >= 4:
                w.append((s_stat, st_base + 8 + (j - 4) + 1))
            op("act", lambda e, xsl=xsl, sqs=sqs: e.activation(sqs, xsl,
                                                               AF.Square),
               waits=w, inc=s_lna, drain=True)
            sq_v.append(s_lna.n)
        for j in range(8):
            sqs = sq_bf[:, (j % 4) * 256:(j % 4) * 256 + 256]
            op("te", lambda e, j=j, sqs=sqs: e.matmul(
                   P6[:, 0:256], ones128[:, :], sqs, start=(j == 0),
                   stop=(j == 7)),
               waits=[(s_lna, sq_v[j])], inc=s_stat)
        stv = s_stat.n
        op("dve", lambda e: e.tensor_single_scalar(mu_bc[:, 0:256], P5[:, 0:256],
                                                   1.0 / D, op=ALU.mult),
           waits=[(s_stat, stv)], drain=True)
        op("dve", lambda e: e.tensor_mul(var_t[:, 0:256], mu_bc[:, 0:256],
                                         mu_bc[:, 0:256]), drain=True)
        op("dve", lambda e: e.scalar_tensor_tensor(
               var_t[:, 0:256], P6[:, 0:256], 1.0 / D, var_t[:, 0:256],
               op0=ALU.mult, op1=ALU.subtract), inc=s_lnd, drain=True)
        lv = s_lnd.n
        release(["p5", "p6"], s_lnd)
        op("act", lambda e: e.activation(var_t[:, 0:256], var_t[:, 0:256],
                                         AF.Ln, bias=eps_sb[:, 0:1]),
           waits=[(s_lnd, lv)], drain=True)
        op("act", lambda e: e.activation(r_bf[:, 0:256], var_t[:, 0:256],
                                         AF.Exp, scale=-0.5),
           inc=s_lna, drain=True)
        rv = s_lna.n
        for (fc, s1v, xsl) in chunks:
            op("dve", lambda e, xsl=xsl: e.tensor_sub(sq_bf[:, 256:512], xsl,
                                                      mu_bc[:, 0:256]),
               waits=[(s_lna, rv)], drain=True)
            op("dve", lambda e, xsl=xsl: e.tensor_mul(xsl, sq_bf[:, 256:512],
                                                      r_bf[:, 0:256]),
               inc=s_s1, drain=True)

    project(0)
    drain_projq()
    for st in range(1, NST + 1):
        at = st - 1
        attend_tile(4 * at + 0, at)
        attend_tile(4 * at + 1, at)
        att_flush()
        wfc_ln(at, 0)
        if st < NST:
            project(st)
            drain_projq()
        attend_tile(4 * at + 2, at)
        attend_tile(4 * at + 3, at)
        att_flush()
        wfc_ln(at, 1)
    X1_DONE = s_s1.n

    RT_REF = [0]

    def router_batch(t8):
        w0 = claim(["r0"]) + [(s_s1, X1_DONE), (d_w, W_BASE)]
        im = 0
        for t in range(4):
            tt = 4 * t8 + t
            for i in range(8):
                def mk(e, i=i, tt=tt, t=t):
                    return e.matmul(
                        P01[:, t * 16:t * 16 + 16],
                        x_bf[:, i * S + tt * 128:i * S + tt * 128 + 128],
                        rw_sb[:, i * 16:i * 16 + 16],
                        start=(i == 0), stop=(i == 7))
                im += 1
                op("te", mk, waits=(w0 if im == 1 else ()),
                   inc=(s_rt if im == 32 else None))
        rv = s_rt.n
        op("dve", lambda e: e.tensor_copy(lg_sb[:, 0:64], P01[:, 0:64]),
           waits=[(s_rt, rv)], inc=s_rtd, drain=True)
        release(["r0"], s_rtd)

        def T4(fn, eng="dve", waits=(), inc=None):
            for t in range(4):
                op(eng, (lambda e, t=t: fn(e, t)),
                   waits=(waits if t == 0 else ()),
                   inc=(inc if t == 3 else None), drain=(t == 0))
        T4(lambda e, t: e.reduce_max(t2a[:, 8 * t:8 * t + 1],
                                     lg_sb[:, 16 * t:16 * t + 16], axis=AX.X))
        T4(lambda e, t: e.tensor_single_scalar(
               t2b[:, 16 * t:16 * t + 16], lg_sb[:, 16 * t:16 * t + 16],
               t2a[:, 8 * t:8 * t + 1], op=ALU.is_ge))
        T4(lambda e, t: e.scalar_tensor_tensor(
               lg_sb[:, 16 * t:16 * t + 16], t2b[:, 16 * t:16 * t + 16],
               -1e30, lg_sb[:, 16 * t:16 * t + 16], op0=ALU.mult,
               op1=ALU.add))
        T4(lambda e, t: e.reduce_max(t2a[:, 8 * t + 1:8 * t + 2],
                                     lg_sb[:, 16 * t:16 * t + 16], axis=AX.X))
        T4(lambda e, t: e.tensor_sub(t2a[:, 8 * t + 2:8 * t + 3],
                                     t2a[:, 8 * t:8 * t + 1],
                                     t2a[:, 8 * t + 1:8 * t + 2]), inc=s_rtd)
        dv = s_rtd.n
        T4(lambda e, t: e.activation(t2a[:, 8 * t + 3:8 * t + 4],
                                     t2a[:, 8 * t + 2:8 * t + 3], AF.Sigmoid),
           eng="act", waits=[(s_rtd, dv)], inc=s_rta)
        av2 = s_rta.n
        T4(lambda e, t: e.tensor_scalar(t2a[:, 8 * t + 4:8 * t + 5],
                                        t2a[:, 8 * t + 3:8 * t + 4], -1.0,
                                        1.0, op0=ALU.mult, op1=ALU.add),
           waits=[(s_rta, av2)])
        T4(lambda e, t: e.tensor_single_scalar(
               lg_sb[:, 16 * t:16 * t + 16], lg_sb[:, 16 * t:16 * t + 16],
               t2a[:, 8 * t + 1:8 * t + 2], op=ALU.is_ge))
        T4(lambda e, t: e.scalar_tensor_tensor(
               comb_sb[:, 16 * t:16 * t + 16], t2b[:, 16 * t:16 * t + 16],
               t2a[:, 8 * t + 3:8 * t + 4], t2b[:, 16 * t:16 * t + 16],
               op0=ALU.mult, op1=ALU.bypass))
        T4(lambda e, t: e.scalar_tensor_tensor(
               comb_bf[:, 16 * t:16 * t + 16], lg_sb[:, 16 * t:16 * t + 16],
               t2a[:, 8 * t + 4:8 * t + 5], comb_sb[:, 16 * t:16 * t + 16],
               op0=ALU.mult, op1=ALU.add), inc=s_rtd)
        dv2 = s_rtd.n
        tw = claim(["p4"]) + [(s_rtd, dv2), (s_ri, 2)]
        for t in range(4):
            op("te", lambda e, t=t: e.transpose(
                   P4[0:16, t * 128:t * 128 + 128],
                   comb_bf[:, 16 * t:16 * t + 16], ident[:, :]),
               waits=(tw if t == 0 else ()), inc=(s_rt if t == 3 else None))
        trv = s_rt.n
        op("act", lambda e, t8=t8: e.copy(
               combT[0:16, t8 * 512:t8 * 512 + 512], P4[0:16, 0:512]),
           waits=[(s_rt, trv)], inc=s_rta)
        release(["p4"], s_rta)
        RT_REF[0] = s_rta.n

    slot_rel = {}
    ew_ready = {}

    def load_expert(i):
        sl = eslot[i % NSLOT]
        dsem = d_es[i % NSLOT]
        w = [(s_s1, X1_DONE)]
        if i % NSLOT in slot_rel:
            w.append((s_grp, slot_rel[i % NSLOT]))
        if i < 4:
            n, hf = i // 2, i % 2
            srcs = (wg_s[n, :, hf * 256:(hf + 1) * 256],
                    wu_s[n, :, hf * 256:(hf + 1) * 256],
                    wd_s[n, hf * 256:(hf + 1) * 256, :])
        else:
            ei = i - 4
            srcs = (wg_r[ei], wu_r[ei], wd_r[ei])
        dma("pool", sl[:, 0:2048].rearrange("p (kc m) -> p kc m", kc=8),
            srcs[0].rearrange("(kc p) m -> p kc m", p=128), dsem, waits=w)
        dma("pool", sl[:, 2048:4096].rearrange("p (kc m) -> p kc m", kc=8),
            srcs[1].rearrange("(kc p) m -> p kc m", p=128), dsem)
        dma("pool", sl[:, 4096:6144].rearrange("p (mc f) -> p mc f", mc=2),
            srcs[2].rearrange("(mc p) f -> p mc f", p=128), dsem)
        ew_ready[i] = (dsem, dsem.n)

    cr_ready = {}
    cb_gate = [0, 0]
    cb_base = [0]

    def ffn_group(gi):
        grp = GROUPS[gi]
        first, last = (gi == 0), (gi == len(GROUPS) - 1)
        routed_g = [ei for ei in grp if ei >= 4]
        cb_base[0] = s_cb.n
        for t8 in range(NFT):
            tok0 = t8 * 512
            if first:
                router_batch(t8)
            if last:
                tl_list = ([0, 1] if NFT > 1 else [0]) if t8 == 0 else (
                    [t8 + 1] if t8 + 1 < NFT else [])
                for tl in tl_list:
                    fas = d_fa2[tl % 2]
                    w = [(d_acc2[0], d_acc2[0].n), (d_acc2[1], d_acc2[1].n)]
                    if tl >= 2:
                        w.append((s_s2, 8 * (tl - 1)))
                    for fc in range(8):
                        dma("sync", acc_ld[:, (tl % 2) * 4096 + fc * 512:
                                           (tl % 2) * 4096 + fc * 512 + 512],
                            facc[fc, :, tl * 512:tl * 512 + 512], fas,
                            waits=w)
                    acc_ready[tl] = (fas, fas.n)
            if routed_g:
                nxt = t8 + 1 if t8 + 1 < NFT else None
                tgts = [t8] if t8 == 0 else ([] if nxt is None else [nxt])
                if t8 == 0 and NFT > 1:
                    tgts = [0, 1]
                for tl in tgts:
                    crs = d_cr2[tl % 2]
                    cw = [(s_rta, RT_REF[0]), (s_cb, cb_gate[tl % 2])]
                    cb_gate[tl % 2] = cb_base[0] + len(routed_g) * (tl + 1)
                    for j, ei in enumerate(grp):
                        if ei < 4:
                            continue
                        dma("sync",
                            comb_rowT[0:1, (tl % 2) * 2048 + j * 512:
                                      (tl % 2) * 2048 + j * 512 + 512],
                            combT[ei - 4:ei - 3, tl * 512:tl * 512 + 512],
                            crs, waits=cw)
                    for ei in routed_g:
                        cr_ready[(ei, tl)] = (crs, crs.n)

            for j, ei in enumerate(grp):
                sl = eslot[ei % NSLOT]
                routed = ei >= 4
                cbs = comb_bc[:, (j % 2) * 512:(j % 2) * 512 + 512]
                if routed:
                    cw = claim(["p7"]) + [cr_ready[(ei, t8)],
                                          (s_cbe, max(0, s_cbe.n - 1))]
                    op("te", lambda e, t8=t8, j=j: e.matmul(
                           P7[:, 0:512], ones1r[:, :],
                           comb_rowT[0:1, (t8 % 2) * 2048 + j * 512:
                                     (t8 % 2) * 2048 + j * 512 + 512],
                           start=True, stop=True),
                       waits=cw, inc=s_cb)
                    cbv = s_cb.n
                    op("act", lambda e, cbs=cbs: e.copy(cbs, P7[:, 0:512]),
                       waits=[(s_cb, cbv)], inc=s_cbe)
                    cbev = s_cbe.n
                    release(["p7"], s_cbe)
                for (dst, woff, smm, res) in ((P01, 0, s_g, ["r0", "r1"]),
                                              (P23, 2048, s_u, ["r2", "r3"])):
                    w0 = claim(res) + [ew_ready[ei],
                                       (s_s1, X1_DONE)]
                    for mc in range(2):
                        for i in range(8):
                            def mk(e, i=i, mc=mc, dst=dst, woff=woff, sl=sl,
                                   tok0=tok0):
                                return e.matmul(
                                    dst[:, mc * 512:mc * 512 + 512],
                                    sl[:, woff + i * 256 + mc * 128:
                                       woff + i * 256 + mc * 128 + 128],
                                    x_bf[:, i * S + tok0:i * S + tok0 + 512],
                                    start=(i == 0), stop=(i == 7))
                            op("te", mk,
                               waits=(w0 if (mc == 0 and i == 0) else ()),
                               inc=(smm if (mc == 1 and i == 7) else None))
                gv, uv = s_g.n, s_u.n
                op("act", lambda e: e.activation(silu_sb[:, :], P01[:, :],
                                                 AF.Silu),
                   waits=[(s_g, gv), (s_h, s_h.n)], inc=s_sil)
                sv_ = s_sil.n
                release(["r0", "r1"], s_sil)
                hsl = h_sb[:, j * 1024:j * 1024 + 1024]
                hw2 = [(s_u, uv), (s_dn, s_dn.n)]
                if routed:
                    for mc in range(2):
                        op("dve", lambda e, mc=mc, hsl=hsl, cbs=cbs:
                               e.scalar_tensor_tensor(
                                   hsl[:, mc * 512:mc * 512 + 512],
                                   P23[:, mc * 512:mc * 512 + 512], 1.0, cbs,
                                   op0=ALU.mult, op1=ALU.mult),
                           waits=(hw2 + [(s_cbe, cbev)] if mc == 0 else ()),
                           inc=(s_usc if mc == 1 else None), drain=(mc == 0))
                else:
                    op("dve", lambda e, hsl=hsl: e.tensor_copy(hsl, P23[:, :]),
                       waits=hw2, inc=s_usc, drain=True)
                uscv = s_usc.n
                release(["r2", "r3"], s_usc)
                op("dve", lambda e, hsl=hsl: e.tensor_mul(hsl, hsl,
                                                          silu_sb[:, :]),
                   waits=[(s_usc, uscv), (s_sil, sv_)], inc=s_h, drain=True)
            hv_all = s_h.n
            bank_ap = {"p5": P5[:, 0:512], "p6": P6[:, 0:512],
                       "p7": P7[:, 0:512],
                       "r0": P01[:, 0:512], "r1": P01[:, 512:1024],
                       "r2": P23[:, 0:512], "r3": P23[:, 512:1024]}
            for fcs, bres in (((0, 1, 2), ("p5", "p6", "p7")),
                              ((3, 4, 5), ("p5", "p6", "p7")),
                              ((6, 7), ("p5", "p6"))):
                nmm = len(fcs) * 2 * len(grp)
                w0 = [(s_h, hv_all)]
                for r in bres:
                    w0 += claim([r])
                im = 0
                for j, ei in enumerate(grp):
                    sl = eslot[ei % NSLOT]
                    for fi, fc in enumerate(fcs):
                        ps_ap = bank_ap[bres[fi]]
                        for mc in range(2):
                            def mk(e, sl=sl, fc=fc, mc=mc, ps_ap=ps_ap, j=j):
                                return e.matmul(
                                    ps_ap,
                                    sl[:, 4096 + mc * 1024 + fc * 128:
                                       4096 + mc * 1024 + fc * 128 + 128],
                                    h_sb[:, j * 1024 + mc * 512:
                                         j * 1024 + mc * 512 + 512],
                                    start=(j == 0 and mc == 0),
                                    stop=(j == len(grp) - 1 and mc == 1))
                            im += 1
                            op("te", mk, waits=(w0 if im == 1 else ()),
                               inc=(s_dn if im == nmm else None))
                dnv = s_dn.n
                for fi, fc in enumerate(fcs):
                    ps = bank_ap[bres[fi]]
                    if last:
                        asl = acc_ld[:, (t8 % 2) * 4096 + fc * 512:
                                     (t8 % 2) * 4096 + fc * 512 + 512]
                        xsl = x_bf[:, fc * S + tok0:fc * S + tok0 + 512]
                        op("dve", lambda e, ps=ps, asl=asl, xsl=xsl:
                               e.scalar_tensor_tensor(xsl, ps, 1.0, asl,
                                                      op0=ALU.mult,
                                                      op1=ALU.add),
                           waits=[(s_dn, dnv),
                                  acc_ready[t8]],
                           inc=s_s2, drain=True)
                        release([bres[fi]], s_s2)
                    else:
                        sgl = stage[:, (fc % 2) * 512:(fc % 2) * 512 + 512]
                        accs = d_acc2[fc % 2]
                        dw2 = [(s_dn, dnv), (accs, accs.n)]
                        if first:
                            xsl = x_bf[:, fc * S + tok0:fc * S + tok0 + 512]
                            op("dve", lambda e, ps=ps, sgl=sgl, xsl=xsl:
                                   e.scalar_tensor_tensor(sgl, ps, 1.0, xsl,
                                                          op0=ALU.mult,
                                                          op1=ALU.add),
                               waits=dw2, inc=s_dne, drain=True)
                        else:
                            op("dve", lambda e, ps=ps, sgl=sgl: e.tensor_copy(
                                   sgl, ps),
                               waits=dw2, inc=s_dne, drain=True)
                        ev = s_dne.n
                        release([bres[fi]], s_dne)
                        dma("pool", facc[fc, :, tok0:tok0 + 512], sgl, accs,
                            waits=[(s_dne, ev)], accum=(not first))
            if last:
                s2v = s_s2.n
                sw = claim(["p5", "p6"])
                st_base = s_stat.n
                for j in range(8):
                    fc = j
                    xsl = x_bf[:, fc * S + tok0:fc * S + tok0 + 512]
                    op("te", lambda e, xsl=xsl, j=j: e.matmul(
                           P5[:, 0:512], ones128[:, :], xsl, start=(j == 0),
                           stop=(j == 7)),
                       waits=([(s_s2, s2v)] + sw if j == 0 else
                              [(s_s2, s2v)]),
                       inc=s_stat)
                sq_v = []
                for j in range(8):
                    fc = j
                    xsl = x_bf[:, fc * S + tok0:fc * S + tok0 + 512]
                    sqs = sq_bf[:, (j % 2) * 512:(j % 2) * 512 + 512]
                    w = [(s_s2, s2v)]
                    if j >= 2:
                        w.append((s_stat, st_base + 8 + (j - 2) + 1))
                    op("act", lambda e, xsl=xsl, sqs=sqs: e.activation(
                           sqs, xsl, AF.Square),
                       waits=w, inc=s_lna, drain=True)
                    sq_v.append(s_lna.n)
                for j in range(8):
                    sqs = sq_bf[:, (j % 2) * 512:(j % 2) * 512 + 512]
                    op("te", lambda e, j=j, sqs=sqs: e.matmul(
                           P6[:, 0:512], ones128[:, :], sqs, start=(j == 0),
                           stop=(j == 7)),
                       waits=[(s_lna, sq_v[j])], inc=s_stat)
                stv = s_stat.n
                op("dve", lambda e: e.tensor_single_scalar(
                       mu_bc[:, :], P5[:, :], 1.0 / D, op=ALU.mult),
                   waits=[(s_stat, stv)], drain=True)
                op("dve", lambda e: e.tensor_mul(var_t[:, :], mu_bc[:, :],
                                                 mu_bc[:, :]), drain=True)
                op("dve", lambda e: e.scalar_tensor_tensor(
                       var_t[:, :], P6[:, :], 1.0 / D, var_t[:, :],
                       op0=ALU.mult, op1=ALU.subtract), inc=s_lnd, drain=True)
                lv = s_lnd.n
                release(["p5", "p6"], s_lnd)
                op("act", lambda e: e.activation(var_t[:, :], var_t[:, :],
                                                 AF.Ln, bias=eps_sb[:, 0:1]),
                   waits=[(s_lnd, lv)], drain=True)
                op("act", lambda e: e.activation(r_bf[:, :], var_t[:, :],
                                                 AF.Exp, scale=-0.5),
                   inc=s_lna, drain=True)
                rv = s_lna.n
                for fc in range(8):
                    xsl = x_bf[:, fc * S + tok0:fc * S + tok0 + 512]
                    osl = out_st2[:, (fc % 2) * 512:(fc % 2) * 512 + 512]
                    ods = d_out2[fc % 2]
                    op("dve", lambda e, xsl=xsl: e.tensor_sub(
                           sq_bf[:, 0:512], xsl, mu_bc[:, :]),
                       waits=[(s_lna, rv), (ods, ods.n)],
                       drain=True)
                    op("dve", lambda e, osl=osl: e.tensor_mul(
                           osl, sq_bf[:, 0:512], r_bf[:, :]),
                       inc=s_x2, drain=True)
                    xv = s_x2.n
                    dma("sync", outT[fc * 128:fc * 128 + 128, tok0:tok0 + 512],
                        osl, ods, waits=[(s_x2, xv)])
        op("te", lambda e: e.nop(), inc=s_grp)
        for ei in grp:
            slot_rel[ei % NSLOT] = s_grp.n

    for i in GROUPS[0]:
        load_expert(i)
    acc_ready = {}
    for gi in range(len(GROUPS)):
        ffn_group(gi)
        if gi + 1 < len(GROUPS):
            for i in GROUPS[gi + 1]:
                load_expert(i)

    P["sync"].append(lambda e: e.wait_ge(d_out2[0].h, d_out2[0].n))
    P["sync"].append(lambda e: e.wait_ge(d_out2[1].h, d_out2[1].n))

    bcm = nc.Block()
    block = bcm.__enter__()

    def runner(eng):
        def f(e):
            for g in P[eng]:
                g(e)
        return f
    block.tensor(runner("te"))
    block.vector(runner("dve"))
    block.scalar(runner("act"))
    block.gpsimd(runner("pool"))
    block.sync(runner("sync"))
    bcm.__exit__(None, None, None)
    return nc


def kernel(L0_attn_x, L0_attn_alpha, L0_attn_Wq, L0_attn_bq, L0_attn_Wk,
           L0_attn_bk, L0_attn_Wv, L0_attn_bv, L0_attn_W_o, L0_attn_Wfc,
           L0_attn_bfc, L0_ffn_router_w, L0_ffn_wg_r, L0_ffn_wu_r,
           L0_ffn_wd_r, L0_ffn_wg_s, L0_ffn_wu_s, L0_ffn_wd_s):
    x = np.asarray(L0_attn_x, dtype=np.float32)
    B, S, _ = x.shape
    f32 = lambda a: np.ascontiguousarray(np.asarray(a, dtype=np.float32))
    xl = f32(x[:, -2:, :].reshape(B, 2 * D))
    alpha = f32(L0_attn_alpha).reshape(1, 1)
    b2 = lambda b: f32(np.asarray(b, dtype=np.float32).reshape(8, 128).T)
    woa = f32(L0_attn_W_o)
    wo2 = np.zeros((128, 1024), np.float32)
    for pp in range(8):
        wo2[0:64, pp * 128:pp * 128 + 64] = woa[2 * pp]
        wo2[64:128, pp * 128 + 64:pp * 128 + 128] = woa[2 * pp + 1]
    wo2 = f32(wo2)
    common = {"xl": xl, "alpha": alpha, "wq": f32(L0_attn_Wq),
              "wk": f32(L0_attn_Wk), "wv": f32(L0_attn_Wv),
              "wfc": f32(L0_attn_Wfc), "bq": b2(L0_attn_bq),
              "bk": b2(L0_attn_bk), "bfc": b2(L0_attn_bfc),
              "bv": b2(L0_attn_bv), "wo2": wo2, "rw": f32(L0_ffn_router_w),
              "wg_r": f32(L0_ffn_wg_r), "wu_r": f32(L0_ffn_wu_r),
              "wd_r": f32(L0_ffn_wd_r), "wg_s": f32(L0_ffn_wg_s),
              "wu_s": f32(L0_ffn_wu_s), "wd_s": f32(L0_ffn_wd_s)}
    nc = build_nc(S)
    in_maps = [dict(common, xT=np.ascontiguousarray(x[c].T)) for c in range(B)]
    res = run_bass_kernel_spmd(nc, in_maps, core_ids=list(range(B)))
    out = np.stack([np.asarray(r["outT"]).T for r in res.results], axis=0)
    return out.astype(np.float32)


# version 22 (final)
# speedup vs baseline: 1.1511x; 1.0019x over previous
import numpy as np
import concourse.bass as bass
import concourse.mybir as mybir
from concourse.bass_utils import run_bass_kernel_spmd

F32 = mybir.dt.float32
BF16 = mybir.dt.bfloat16
AF = mybir.ActivationFunctionType
ALU = mybir.AluOpType
AX = mybir.AxisListType

D = 1024
RING = 8
DVE_DRAIN = True
DT_B = {F32: 4, BF16: 2}
GROUPS = [list(range(4 * g, 4 * g + 4)) for g in range(5)]
NSLOT = 6


class Sem:
    def __init__(self, nc, name):
        self.h = nc.alloc_semaphore(name)
        self.n = 0


def build_nc(S=4096):
    NT = S // 128
    NST = S // 512
    NFT = S // 512

    nc = bass.Bass("TRN2", target_bir_lowering=False)

    din = lambda n, s: nc.dram_tensor(n, list(s), F32, kind="ExternalInput").ap()
    xT = din("xT", [D, S])
    xl_d = din("xl", [8, 2 * D])
    alpha_d = din("alpha", [1, 1])
    wq_d, wk_d, wv_d, wfc_d = (din(n, [D, D]) for n in ("wq", "wk", "wv", "wfc"))
    bq_d, bk_d, bfc_d, bv_d = (din(n, [128, 8]) for n in ("bq", "bk", "bfc", "bv"))
    wo_d = din("wo2", [128, 1024])
    rw_d = din("rw", [D, 16])
    wg_r = din("wg_r", [16, D, 256])
    wu_r = din("wu_r", [16, D, 256])
    wd_r = din("wd_r", [16, 256, D])
    wg_s = din("wg_s", [2, D, 512])
    wu_s = din("wu_s", [2, D, 512])
    wd_s = din("wd_s", [2, 512, D])
    outT = nc.dram_tensor("outT", [D, S], F32, kind="ExternalOutput").ap()
    facc = nc.dram_tensor("facc", [8, 128, S], BF16, kind="Internal").ap()

    SB_BASE = 17408
    cur = [SB_BASE]

    def A(name, shape, dtype):
        nb = int(np.prod(shape[1:])) * DT_B[dtype]
        off = (cur[0] + 63) // 64 * 64
        t = nc.alloc_sbuf_tensor_at(name, list(shape), dtype, offset=off)
        cur[0] = off + nb
        return t

    x_bf = A("x_bf", [128, 8 * S], BF16)
    mask_mid = A("mask_mid", [128, 768], BF16)
    maskb_e0 = A("maskb_e0", [128, 768], BF16)
    maskb_e1 = A("maskb_e1", [128, 768], BF16)
    ones128 = A("ones128", [128, 128], BF16)
    ident = A("ident", [128, 128], BF16)
    ones1r = A("ones1r", [1, 128], BF16)
    ones8f = A("ones8f", [8, 128], F32)
    ones1f = A("ones1f", [1, 128], F32)
    bq8 = A("bq8", [128, 8], F32)
    bk_s = A("bk_s", [128, 8], F32)
    bfc_s = A("bfc_s", [128, 8], F32)
    bv_s = A("bv_s", [128, 8], F32)
    hh_sb = A("hh_sb", [128, 2], F32)
    sct = A("sct", [128, 8], F32)
    alpha_s = A("alpha_s", [1, 1], F32)
    eps_sb = A("eps_sb", [128, 1], F32)
    negmax = A("negmax", [128, 8], F32)
    den = A("den", [128, 16], F32)
    rden = A("rden", [128, 16], F32)
    mu_bc = A("mu_bc", [128, 512], BF16)
    var_t = A("var_t", [128, 512], F32)
    r_bf = A("r_bf", [128, 512], BF16)
    sq_bf = A("sq_bf", [128, 1024], BF16)
    rw_sb = A("rw_sb", [128, 8 * 16], BF16)
    PERS_END = cur[0]

    OV = (PERS_END + 63) // 64 * 64
    cur[0] = OV
    wq_sb = A("wq_sb", [128, 8 * 1024], BF16)
    wk_sb = A("wk_sb", [128, 8 * 1024], BF16)
    wv_sb = A("wv_sb", [128, 8 * 1024], BF16)
    wfc_sb = A("wfc_sb", [128, 8 * 1024], BF16)
    wo_sb = A("wo_sb", [128, 1024], BF16)
    k_ring = A("k_ring", [128, 8 * RING * 128], BF16)
    v_ring = A("v_ring", [128, RING * 1024], BF16)
    q_bf = A("q_bf", [128, 2 * 8 * 512], BF16)
    E_sb = A("E_sb", [128, 4 * 768], BF16)
    AT_sb = A("AT_sb", [128, 3 * 768], BF16)
    ao_bf = A("ao_bf", [128, 2 * 128], BF16)
    out2_bf = A("out2_bf", [128, 8 * 256], BF16)
    ATTN_END = cur[0]

    cur[0] = OV
    eslot = [A(f"eslot{i}", [128, 6144], BF16) for i in range(NSLOT)]
    combT = A("combT", [16, S], BF16)
    comb_rowT = A("comb_rowT", [1, 2 * 4 * 512], BF16)
    comb_bc = A("comb_bc", [128, 2 * 512], BF16)
    silu_sb = A("silu_sb", [128, 1024], BF16)
    h_sb = A("h_sb", [128, 4 * 1024], BF16)
    stage = A("stage", [128, 2 * 512], BF16)
    acc_ld = A("acc_ld", [128, 2 * 8 * 512], BF16)
    lg_sb = A("lg_sb", [128, 64], F32)
    t2a = A("t2a", [128, 32], F32)
    t2b = A("t2b", [128, 64], F32)
    comb_sb = A("comb_sb", [128, 64], F32)
    comb_bf = A("comb_bf", [128, 64], BF16)
    out_st2 = A("out_st2", [128, 2 * 512], F32)
    FFN_END = cur[0]

    LIMIT = 229216
    assert max(ATTN_END, FFN_END) <= LIMIT, (PERS_END, ATTN_END, FFN_END)

    klv = k_ring[:, :].bitcast(F32)
    xl_t = klv[0:8, 0:2 * D]
    d8_t = klv[0:8, 2 * D:3 * D]
    ds_t = klv[0:8, 3 * D:3 * D + 1]
    assert 3 * D + 1 <= RING * 8 * 128 // 2
    DIFF = AT_sb[:, :].bitcast(F32)[:, 0:768]

    P01 = nc.alloc_psum_tensor("P01", [128, 1024], F32)
    P23 = nc.alloc_psum_tensor("P23", [128, 1024], F32)
    P4 = nc.alloc_psum_tensor("P4", [128, 1024], BF16)
    P5 = nc.alloc_psum_tensor("P5", [128, 512], F32)
    P6 = nc.alloc_psum_tensor("P6", [128, 512], F32)
    P7 = nc.alloc_psum_tensor("P7", [128, 512], F32)

    ENGS = ("te", "dve", "act", "pool", "sync")
    P = {e: [] for e in ENGS}
    seen = {e: {} for e in ENGS}

    def sems(*names):
        return [Sem(nc, n) for n in names]

    (d_x, d_w, d_ld, d_out2, d_acc2, d_cr2, d_fa2) = (
        sems("d_x", "d_w", "d_ld")
        + [sems("d_outa", "d_outb"), sems("d_aca", "d_acb"),
           sems("d_cra", "d_crb"), sems("d_faa", "d_fab")])
    d_es = sems(*[f"d_es{i}" for i in range(NSLOT)])
    (s_pj, s_pjev, s_sc, s_mx, s_ex, s_A, s_tr, s_atev, s_av, s_aoev, s_wo,
     s_o2ev, s_wfc, s_s1, s_stat, s_lnd, s_lna, s_hh, s_msk, s_ri) = sems(
        "s_pj", "s_pjev", "s_sc", "s_mx", "s_ex", "s_A", "s_tr", "s_atev",
        "s_av", "s_aoev", "s_wo", "s_o2ev", "s_wfc", "s_s1", "s_stat",
        "s_lnd", "s_lna", "s_hh", "s_msk", "s_ri")
    (s_dva, s_dvb) = sems("s_dva", "s_dvb")
    (s_rt, s_rtd, s_rta, s_g, s_u, s_cb, s_cbe, s_sil, s_usc, s_h, s_dn,
     s_dne, s_grp, s_s2, s_x2) = sems(
        "s_rt", "s_rtd", "s_rta", "s_g", "s_u", "s_cb", "s_cbe", "s_sil",
        "s_usc", "s_h", "s_dn", "s_dne", "s_grp", "s_s2", "s_x2")

    def op(eng, fn, inc=None, inc_n=1, waits=(), drain=False):
        w = []
        for (s, v) in waits:
            if v > 0 and seen[eng].get(s, -1) < v:
                seen[eng][s] = v
                w.append((s.h, v))
        if drain and (eng != "dve" or DVE_DRAIN):
            P[eng].append(lambda e: e.drain())

        def g(e, fn=fn, w=tuple(w), inc=inc, inc_n=inc_n):
            for (h, v) in w:
                e.wait_ge(h, v)
            r = fn(e)
            if inc is not None:
                r.then_inc(inc.h, inc_n)
        P[eng].append(g)
        if inc is not None:
            inc.n += inc_n
        return inc.n if inc is not None else None

    def dma(eng, out, in_, inc, waits=(), accum=False, maxdim=None):
        def f(e, out=out, in_=in_, accum=accum, maxdim=maxdim):
            kw = {}
            if maxdim is not None:
                kw["max_dma_last_dim"] = maxdim
            if accum:
                return e.dma_start(out=out, in_=in_, accum_op=ALU.add, **kw)
            return e.dma_start(out=out, in_=in_, **kw)
        return op(eng, f, inc=inc, inc_n=16, waits=waits)

    rel = {}

    def claim(res_list):
        w = []
        for r in res_list:
            if r in rel:
                w.append(rel[r])
        return w

    def release(res_list, sem):
        for r in res_list:
            rel[r] = (sem, sem.n)
    ROT = [(P01, 0), (P01, 512), (P23, 0), (P23, 512)]
    rot_i = [0]

    def dma_x(sx, w_):
        dma("pool",
            x_bf[:, :].rearrange("p (c t) -> p c t", c=8)[:, :, sx * w_:
                                                          (sx + 1) * w_],
            xT.rearrange("(c p) t -> p c t", p=128)[:, :, sx * w_:
                                                    (sx + 1) * w_], d_x)

    def dma_w(dst, src):
        dma("pool", dst[:, :].rearrange("p (c f) -> p c f", c=8),
            src.rearrange("(c p) f -> p c f", p=128), d_w)

    XW = S // 4
    dma_w(wq_sb, wq_d)
    WQ_N = d_w.n
    dma_x(0, XW)
    dma_w(wk_sb, wk_d)
    WK_N = d_w.n
    dma_w(wv_sb, wv_d)
    WV_N = d_w.n
    dma_x(1, XW)
    dma_x(2, XW)
    dma_w(wfc_sb, wfc_d)
    WFC_N = d_w.n
    dma_x(3, XW)
    D_X_ALL = d_x.n
    X_N = [16, 32, 48, 64]
    dma("pool", wo_sb[:, :], wo_d, d_w)
    WO_N = d_w.n
    dma("pool", rw_sb[:, :].rearrange("p (c f) -> p c f", c=8),
        rw_d.rearrange("(c p) f -> p c f", p=128), d_w)
    W_BASE = d_w.n

    for (dst, src) in ((bq8, bq_d), (bk_s, bk_d), (bfc_s, bfc_d), (bv_s, bv_d),
                       (alpha_s, alpha_d)):
        dma("sync", dst[:, :], src, d_ld)
    dma("sync", xl_t, xl_d, d_ld)
    LD_ALL = d_ld.n

    op("dve", lambda e: e.memset(ones128[:, :], 1.0))
    op("dve", lambda e: e.memset(ones1r[:, :], 1.0))
    op("dve", lambda e: e.memset(ones8f[:, :], 1.0))
    op("dve", lambda e: e.memset(ones1f[:, :], 1.0))
    op("dve", lambda e: e.memset(eps_sb[:, :], 1e-5))
    op("dve", lambda e: e.memset(v_ring[:, :], 0.0), inc=s_ri)
    op("pool", lambda e: e.iota(ident[:, :], pattern=[[-1, 128]], base=0,
                                channel_multiplier=1,
                                allow_small_or_imprecise_dtypes=True),
       inc=s_msk)
    op("pool", lambda e: e.iota(DIFF, pattern=[[0, 2], [-1, 384]], base=128,
                                channel_multiplier=1,
                                allow_small_or_imprecise_dtypes=True),
       inc=s_msk)
    op("dve", lambda e: e.tensor_single_scalar(ident[:, :], ident[:, :], 0.0,
                                               op=ALU.is_equal),
       waits=[(s_msk, 1)], inc=s_ri, drain=True)

    op("dve", lambda e: e.tensor_sub(d8_t, xl_t[:, 0:D], xl_t[:, D:2 * D]),
       waits=[(d_ld, LD_ALL)], inc=s_hh, drain=True)
    op("act", lambda e: e.activation(d8_t, d8_t, AF.Abs),
       waits=[(s_hh, 1)], inc=s_hh)
    op("dve", lambda e: e.reduce_sum(ds_t, d8_t, axis=AX.X),
       waits=[(s_hh, 2)], inc=s_hh, drain=True)
    hw = claim(["p7"])
    op("te", lambda e: e.matmul(P7[:, 0:1], ones8f[:, :], ds_t, start=True,
                                stop=True),
       waits=[(s_hh, 3), (s_ri, 2)] + hw)
    op("te", lambda e: e.matmul(P7[:, 1:2], ones1f[:, :], d8_t[0:1, 0:1],
                                start=True, stop=True))
    op("te", lambda e: e.matmul(P7[:, 2:3], ones1f[:, :], alpha_s[0:1, 0:1],
                                start=True, stop=True),
       waits=[(d_ld, LD_ALL)], inc=s_hh)
    op("dve", lambda e: e.tensor_single_scalar(sct[:, 0:1], P7[:, 0:1],
                                               1.0 / (8 * D), op=ALU.mult),
       waits=[(s_hh, 4)], drain=True)
    op("act", lambda e: e.activation(sct[:, 1:2], P7[:, 2:3], AF.Sigmoid),
       waits=[(s_hh, 4)], inc=s_hh)
    op("dve", lambda e: e.tensor_sub(sct[:, 2:3], sct[:, 0:1], P7[:, 1:2]),
       drain=True)
    op("dve", lambda e: e.tensor_mul(sct[:, 2:3], sct[:, 1:2], sct[:, 2:3]),
       waits=[(s_hh, 5)], drain=True)
    op("dve", lambda e: e.tensor_add(sct[:, 2:3], sct[:, 2:3], P7[:, 1:2]),
       inc=s_hh, drain=True)
    release(["p7"], s_hh)
    op("act", lambda e: e.activation(sct[:, 3:4], sct[:, 2:3], AF.Sigmoid),
       waits=[(s_hh, 6)], inc=s_hh, drain=True)
    op("dve", lambda e: e.tensor_scalar(hh_sb[:, 0:1], sct[:, 3:4], 48.0, 12.0,
                                        op0=ALU.mult, op1=ALU.add),
       waits=[(s_hh, 7)], drain=True)
    op("dve", lambda e: e.tensor_single_scalar(hh_sb[:, 1:2], hh_sb[:, 0:1],
                                               -1.0, op=ALU.mult),
       inc=s_hh, drain=True)
    m1s = E_sb[:, 0:768]
    op("dve", lambda e: e.tensor_single_scalar(m1s, DIFF, hh_sb[:, 0:1],
                                               op=ALU.is_le),
       waits=[(s_hh, 8), (s_msk, 2)], drain=True)
    op("dve", lambda e: e.tensor_single_scalar(mask_mid[:, :], DIFF,
                                               hh_sb[:, 1:2], op=ALU.is_ge),
       drain=True)
    op("dve", lambda e: e.tensor_mul(mask_mid[:, :], mask_mid[:, :], m1s),
       drain=True)
    op("dve", lambda e: e.tensor_scalar(mask_mid[:, :], mask_mid[:, :], 100.0,
                                        -100.0, op0=ALU.mult, op1=ALU.add),
       drain=True)
    op("dve", lambda e: e.tensor_copy(maskb_e0[:, :], mask_mid[:, :]),
       drain=True)
    op("dve", lambda e: e.tensor_copy(maskb_e1[:, :], mask_mid[:, :]),
       drain=True)
    op("dve", lambda e: e.memset(
        maskb_e0[:, :].rearrange("p (h w) -> p h w", h=2)[:, :, 0:128], -100.0),
       drain=True)
    op("dve", lambda e: e.memset(
        maskb_e1[:, :].rearrange("p (h w) -> p h w", h=2)[:, :, 256:384],
        -100.0), inc=s_msk, drain=True)
    MSK_DONE = s_msk.n
    op("dve", lambda e: e.memset(k_ring[:, :], 0.0),
       waits=[(s_hh, 4)], inc=s_ri, drain=True)
    RI_DONE = s_ri.n
    op("dve", lambda e: e.tensor_single_scalar(bq8[:, :], bq8[:, :], 0.125,
                                               op=ALU.mult),
       waits=[(d_ld, LD_ALL)], inc=s_hh, drain=True)
    HH_DONE = s_hh.n

    PROJQ = []

    def proj_set(mk_mm, evacs, extra_waits=()):
        PROJQ.append((mk_mm, evacs, list(extra_waits)))

    def emit_proj_set():
        if not PROJQ:
            return
        mk_mm, evacs, extra = PROJQ.pop(0)
        ri = rot_i[0] % 4
        rot_i[0] += 1
        ps_t, ps_o = ROT[ri]
        ps = ps_t[:, ps_o:ps_o + 512]
        w = claim([f"r{ri}"]) + extra
        for i in range(8):
            op("te", (lambda e, i=i, ps=ps: mk_mm(e, i, ps)),
               waits=(w if i == 0 else ()), inc=(s_pj if i == 7 else None))
        mmv = s_pj.n
        for ev in evacs:
            op("act", (lambda e, ev=ev, ps=ps: ev(e, ps)),
               waits=[(s_pj, mmv)], inc=s_pjev)
        release([f"r{ri}"], s_pjev)

    def drain_projq():
        while PROJQ:
            emit_proj_set()

    q_ready = {}
    k_ready = {}

    def project(st):
        base_w = [(d_x, X_N[st // 2]), (s_ri, RI_DONE), (s_hh, HH_DONE)]
        for fc in range(8):
            def mk(e, i, ps, fc=fc):
                return e.matmul(ps, wq_sb[:, i * 1024 + fc * 128:
                                          i * 1024 + fc * 128 + 128],
                                x_bf[:, i * S + st * 512:i * S + st * 512 + 512],
                                start=(i == 0), stop=(i == 7))

            def ev(e, ps, fc=fc):
                return e.activation(
                    q_bf[:, (st % 2) * 4096 + fc * 512:
                         (st % 2) * 4096 + fc * 512 + 512],
                    ps, AF.Identity, bias=bq8[:, fc:fc + 1], scale=0.125)
            proj_set(mk, [ev], extra_waits=base_w + [(d_w, WQ_N)])
        for fc in range(8):
            def mk(e, i, ps, fc=fc):
                return e.matmul(ps, wk_sb[:, i * 1024 + fc * 128:
                                          i * 1024 + fc * 128 + 128],
                                x_bf[:, i * S + st * 512:i * S + st * 512 + 512],
                                start=(i == 0), stop=(i == 7))
            evs = []
            for tt in range(4):
                slot = (4 * st + tt) % RING

                def ev(e, ps, fc=fc, tt=tt, slot=slot):
                    return e.activation(
                        k_ring[:, fc * RING * 128 + slot * 128:
                               fc * RING * 128 + slot * 128 + 128],
                        ps[:, tt * 128:tt * 128 + 128], AF.Identity,
                        bias=bk_s[:, fc:fc + 1], scale=1.0)
                evs.append(ev)
            proj_set(mk, evs, extra_waits=base_w + [(d_w, WK_N)])
        for tt in range(4):
            T = 4 * st + tt
            slot = T % RING
            for fh in range(2):
                def mk(e, i, ps, T=T, fh=fh):
                    return e.matmul(ps, x_bf[:, i * S + T * 128:
                                             i * S + T * 128 + 128],
                                    wv_sb[:, i * 1024 + fh * 512:
                                          i * 1024 + fh * 512 + 512],
                                    start=(i == 0), stop=(i == 7))

                def ev(e, ps, slot=slot, fh=fh):
                    return e.copy(v_ring[:, slot * 1024 + fh * 512:
                                         slot * 1024 + fh * 512 + 512], ps)
                proj_set(mk, [ev], extra_waits=base_w + [(d_w, WV_N)])
        done = s_pjev.n + sum(len(ev) for (_, ev, _) in PROJQ)
        q_ready[st % 2] = done
        for tt in range(4):
            k_ready[4 * st + tt + 1] = done
        k_ready[0] = k_ready.get(0, done)

    SCRES = [["r0", "r1"], ["r2", "r3"]]
    SC = [P01, P23]

    ATTST = {}
    GP = [0]
    E_LAST = {}
    AT_LAST = {}

    def emit_round(new_s, l1, l2, l3):
        if new_s is not None:
            s = new_s
            p, scp, par, T = s["p"], s["scp"], s["par"], s["T"]
            w0 = claim(SCRES[s["g"] % 2]) + [
                (s_pjev, q_ready[par]), (s_pjev, k_ready[min(T + 1, NT)]),
                (s_msk, MSK_DONE)]
            runs = []
            c = 0
            while c < 3:
                slot = (T - 1 + c) % RING
                w_ = 1
                while c + w_ < 3 and (T - 1 + c + w_) % RING == slot + w_:
                    w_ += 1
                runs.append((c, slot, w_))
                c += w_
            nmm = 2 * len(runs)
            im = 0
            for hi in range(2):
                b = hi * 64
                for (c0, slot, w_) in runs:
                    def mk(e, c0=c0, b=b, slot=slot, w_=w_, p=p, scp=scp,
                           par=par, T=T):
                        return e.matmul(
                            scp[:, (b // 64) * 512 + c0 * 128:
                                (b // 64) * 512 + c0 * 128 + 128 * w_],
                            q_bf[b:b + 64, par * 4096 + p * 512 + (T % 4) * 128:
                                 par * 4096 + p * 512 + (T % 4) * 128 + 128],
                            k_ring[b:b + 64, p * RING * 128 + slot * 128:
                                   p * RING * 128 + slot * 128 + 128 * w_],
                            start=True, stop=True)
                    im += 1
                    op("te", mk, waits=(w0 if im == 1 else ()),
                       inc=(s_sc if im == nmm else None))
            s["scv"] = s_sc.n
            segs = scp[:, :].rearrange("q (h w) -> q h w", h=2)[:, :, 0:384]
            s["segs"] = segs
        if l1 is not None:
            s = l1
            op("dve", lambda e, s=s: e.reciprocal(
                   rden[:, 2 * s["p8"]:2 * s["p8"] + 2],
                   den[:, 2 * s["p8"]:2 * s["p8"] + 2]),
               waits=[(s_ex, s["exv"])], inc=s_dvb)
            s["dvb"] = s_dvb.n
            op("dve", lambda e, s=s: e.tensor_single_scalar(
                   s["E"][:, 0:384], s["E"][:, 0:384],
                   rden[:, 2 * s["p8"]:2 * s["p8"] + 1], op=ALU.mult),
               waits=[(s_dvb, s["dvb"])])
            op("dve", lambda e, s=s: e.tensor_single_scalar(
                   s["E"][:, 384:768], s["E"][:, 384:768],
                   rden[:, 2 * s["p8"] + 1:2 * s["p8"] + 2], op=ALU.mult),
               inc=s_A)
            s["avl"] = s_A.n
            tw = claim(["p4"]) + [(s_A, s["avl"]), (s_ri, 2)]
            for c in range(6):
                op("te", lambda e, s=s, c=c: e.transpose(
                       P4[:, c * 128:c * 128 + 128],
                       s["E"][:, c * 128:c * 128 + 128], ident[:, :]),
                   waits=(tw if c == 0 else ()),
                   inc=(s_tr if c == 5 else None))
            s["trv"] = s_tr.n
            E_LAST[s["g"] % 4] = s["trv"]
        if l3 is not None:
            s = l3
            o2w = [(s_wo, s["wov"]), (s_wfc, s_wfc.n)]
            op("act", lambda e, s=s: e.copy(
                   out2_bf[:, s["o2c"]:s["o2c"] + 128], P7[:, 0:128]),
               waits=o2w, inc=s_o2ev)
            release(["p7"], s_o2ev)
        if l1 is not None:
            s = l1
            atw = [(s_tr, s["trv"])]
            if s["g"] % 3 in AT_LAST:
                atw.append((s_av, AT_LAST[s["g"] % 3]))
            op("act", lambda e, s=s: e.copy(s["AT"], P4[:, 0:768]),
               waits=atw, inc=s_atev)
            s["atv"] = s_atev.n
            release(["p4"], s_atev)
            lo = (s["g"] % 2 == 0)
            aw = claim(["p5" if lo else "p6"]) + [(s_atev, s["atv"])]
            for c in range(3):
                slot = (s["T"] - 1 + c) % RING
                for hi in range(2):
                    def mk(e, c=c, hi=hi, slot=slot, s=s, lo=lo):
                        dst = (P5[hi * 64:hi * 64 + 64, 0:128] if lo
                               else P6[hi * 64:hi * 64 + 64, 128:256])
                        return e.matmul(
                            dst,
                            v_ring[:, slot * 1024 + (2 * s["p"] + hi) * 64:
                                   slot * 1024 + (2 * s["p"] + hi) * 64 + 64],
                            s["AT"][:, hi * 384 + c * 128:
                                    hi * 384 + c * 128 + 128],
                            start=(c == 0), stop=(c == 2),
                            tile_position=(0, hi * 64), skip_group_check=True)
                    op("te", mk, waits=(aw if (c == 0 and hi == 0) else ()),
                       inc=(s_av if (c == 2 and hi == 1) else None))
            s["avv"] = s_av.n
            AT_LAST[s["g"] % 3] = s["avv"]
        if l2 is not None:
            s = l2
            lo = (s["g"] % 2 == 0)
            src_ps = P5[:, 0:128] if lo else P6[:, 128:256]
            op("act", lambda e, s=s, src_ps=src_ps: e.activation(
                   s["ao"], src_ps, AF.Identity,
                   bias=bv_s[:, s["p8"]:s["p8"] + 1], scale=1.0),
               waits=[(s_av, s["avv"])], inc=s_aoev)
            s["aov"] = s_aoev.n
            release(["p5" if (s["g"] % 2 == 0) else "p6"], s_aoev)
            ww = claim(["p7"]) + [(s_aoev, s["aov"]), (d_w, WO_N)]
            op("te", lambda e, s=s: e.matmul(
                   P7[:, 0:128],
                   wo_sb[:, s["p"] * 128:s["p"] * 128 + 128],
                   s["ao"][:, :], start=True, stop=True),
               waits=ww, inc=s_wo)
            s["wov"] = s_wo.n
        if new_s is not None:
            s = new_s
            op("dve", lambda e, segs=s["segs"], mb=s["mb"]:
                   e.scalar_tensor_tensor(segs, segs, 1.0, mb[:, :],
                                          op0=ALU.mult, op1=ALU.add),
               waits=[(s_sc, s["scv"])], inc=s_dva)
            s["dva"] = s_dva.n
            op("dve", lambda e, s=s: e.reduce_max(
                   negmax[:, s["p8"]:s["p8"] + 1], s["segs"], axis=AX.XY,
                   negate=True),
               waits=[(s_dva, s["dva"])], inc=s_mx)
            s["mxv"] = s_mx.n
            ew = []
            if s["g"] % 4 in E_LAST:
                ew = [(s_tr, E_LAST[s["g"] % 4])]
            op("act", lambda e, s=s: e.activation(
                   s["E"][:, 0:384], s["scp"][:, 0:384], AF.Exp,
                   bias=negmax[:, s["p8"]:s["p8"] + 1],
                   accum_out=den[:, 2 * s["p8"]:2 * s["p8"] + 1]),
               waits=[(s_mx, s["mxv"])] + ew)
            op("act", lambda e, s=s: e.activation(
                   s["E"][:, 384:768], s["scp"][:, 512:896], AF.Exp,
                   bias=negmax[:, s["p8"]:s["p8"] + 1],
                   accum_out=den[:, 2 * s["p8"] + 1:2 * s["p8"] + 2]),
               inc=s_ex)
            s["exv"] = s_ex.n
            release(SCRES[s["g"] % 2], s_ex)
    def att_step(new_s):
        g = GP[0]
        l1 = ATTST.get(g - 1)
        l2 = ATTST.get(g - 2)
        l3 = ATTST.get(g - 3)
        emit_round(new_s, l1, l2, l3)
        if new_s is not None:
            ATTST[g] = new_s
            GP[0] += 1

    def attend_tile(T, st):
        par = st % 2
        maskb = mask_mid if (0 < T < NT - 1) else (maskb_e0 if T == 0
                                                   else maskb_e1)
        for p in range(8):
            g = GP[0]
            s = {"p": p, "p8": p, "T": T, "par": par, "g": g, "mb": maskb,
                 "scp": SC[g % 2],
                 "E": E_sb[:, (g % 4) * 768:(g % 4) * 768 + 768],
                 "AT": AT_sb[:, (g % 3) * 768:(g % 3) * 768 + 768],
                 "ao": ao_bf[:, (g % 2) * 128:(g % 2) * 128 + 128],
                 "o2c": p * 256 + (T % 2) * 128}
            att_step(s)

    def att_flush():
        for _ in range(3):
            g = GP[0]
            emit_round(None, ATTST.get(g - 1), ATTST.get(g - 2),
                       ATTST.get(g - 3))
            GP[0] += 1

    def wfc_ln(st, half):
        tok0 = st * 512 + half * 256
        o2v = s_o2ev.n
        chunks = []
        for fc in range(8):
            ri = rot_i[0] % 4
            rot_i[0] += 1
            ps_t, ps_o = ROT[ri]
            ps = ps_t[:, ps_o:ps_o + 256]
            w0 = claim([f"r{ri}"]) + [(s_o2ev, o2v), (d_w, WFC_N)]
            for i in range(8):
                def mk(e, i=i, fc=fc, ps=ps):
                    return e.matmul(ps, wfc_sb[:, i * 1024 + fc * 128:
                                               i * 1024 + fc * 128 + 128],
                                    out2_bf[:, i * 256:i * 256 + 256],
                                    start=(i == 0), stop=(i == 7))
                op("te", mk, waits=(w0 if i == 0 else ()),
                   inc=(s_wfc if i == 7 else None))
            mmv = s_wfc.n
            xsl = x_bf[:, fc * S + tok0:fc * S + tok0 + 256]
            op("dve", lambda e, ps=ps, xsl=xsl, fc=fc: e.scalar_tensor_tensor(
                   xsl, ps, bfc_s[:, fc:fc + 1], xsl, op0=ALU.add, op1=ALU.add),
               waits=[(s_wfc, mmv)], inc=s_s1, drain=True)
            release([f"r{ri}"], s_s1)
            chunks.append((fc, s_s1.n, xsl))
        sw = claim(["p5", "p6"])
        st_base = s_stat.n
        for j, (fc, s1v, xsl) in enumerate(chunks):
            op("te", lambda e, xsl=xsl, j=j: e.matmul(
                   P5[:, 0:256], ones128[:, :], xsl, start=(j == 0),
                   stop=(j == 7)),
               waits=([(s_s1, s1v)] + sw if j == 0 else [(s_s1, s1v)]),
               inc=s_stat)
        sq_v = []
        for j, (fc, s1v, xsl) in enumerate(chunks):
            sqs = sq_bf[:, (j % 4) * 256:(j % 4) * 256 + 256]
            w = [(s_s1, s1v)]
            if j >= 4:
                w.append((s_stat, st_base + 8 + (j - 4) + 1))
            op("act", lambda e, xsl=xsl, sqs=sqs: e.activation(sqs, xsl,
                                                               AF.Square),
               waits=w, inc=s_lna, drain=True)
            sq_v.append(s_lna.n)
        for j in range(8):
            sqs = sq_bf[:, (j % 4) * 256:(j % 4) * 256 + 256]
            op("te", lambda e, j=j, sqs=sqs: e.matmul(
                   P6[:, 0:256], ones128[:, :], sqs, start=(j == 0),
                   stop=(j == 7)),
               waits=[(s_lna, sq_v[j])], inc=s_stat)
        stv = s_stat.n
        op("dve", lambda e: e.tensor_single_scalar(mu_bc[:, 0:256], P5[:, 0:256],
                                                   1.0 / D, op=ALU.mult),
           waits=[(s_stat, stv)], drain=True)
        op("dve", lambda e: e.tensor_mul(var_t[:, 0:256], mu_bc[:, 0:256],
                                         mu_bc[:, 0:256]), drain=True)
        op("dve", lambda e: e.scalar_tensor_tensor(
               var_t[:, 0:256], P6[:, 0:256], 1.0 / D, var_t[:, 0:256],
               op0=ALU.mult, op1=ALU.subtract), inc=s_lnd, drain=True)
        lv = s_lnd.n
        release(["p5", "p6"], s_lnd)
        op("act", lambda e: e.activation(var_t[:, 0:256], var_t[:, 0:256],
                                         AF.Ln, bias=eps_sb[:, 0:1]),
           waits=[(s_lnd, lv)], drain=True)
        op("act", lambda e: e.activation(r_bf[:, 0:256], var_t[:, 0:256],
                                         AF.Exp, scale=-0.5),
           inc=s_lna, drain=True)
        rv = s_lna.n
        for (fc, s1v, xsl) in chunks:
            op("dve", lambda e, xsl=xsl: e.tensor_sub(sq_bf[:, 256:512], xsl,
                                                      mu_bc[:, 0:256]),
               waits=[(s_lna, rv)], drain=True)
            op("dve", lambda e, xsl=xsl: e.tensor_mul(xsl, sq_bf[:, 256:512],
                                                      r_bf[:, 0:256]),
               inc=s_s1, drain=True)

    project(0)
    drain_projq()
    for st in range(1, NST + 1):
        at = st - 1
        attend_tile(4 * at + 0, at)
        attend_tile(4 * at + 1, at)
        att_flush()
        wfc_ln(at, 0)
        if st < NST:
            project(st)
            drain_projq()
        attend_tile(4 * at + 2, at)
        attend_tile(4 * at + 3, at)
        att_flush()
        wfc_ln(at, 1)
    X1_DONE = s_s1.n

    RT_REF = [0]

    def router_batch(t8):
        w0 = claim(["r0"]) + [(s_s1, X1_DONE), (d_w, W_BASE)]
        im = 0
        for t in range(4):
            tt = 4 * t8 + t
            for i in range(8):
                def mk(e, i=i, tt=tt, t=t):
                    return e.matmul(
                        P01[:, t * 16:t * 16 + 16],
                        x_bf[:, i * S + tt * 128:i * S + tt * 128 + 128],
                        rw_sb[:, i * 16:i * 16 + 16],
                        start=(i == 0), stop=(i == 7))
                im += 1
                op("te", mk, waits=(w0 if im == 1 else ()),
                   inc=(s_rt if im == 32 else None))
        rv = s_rt.n
        op("dve", lambda e: e.tensor_copy(lg_sb[:, 0:64], P01[:, 0:64]),
           waits=[(s_rt, rv)], inc=s_rtd, drain=True)
        release(["r0"], s_rtd)

        def T4(fn, eng="dve", waits=(), inc=None):
            for t in range(4):
                op(eng, (lambda e, t=t: fn(e, t)),
                   waits=(waits if t == 0 else ()),
                   inc=(inc if t == 3 else None), drain=(t == 0))
        T4(lambda e, t: e.reduce_max(t2a[:, 8 * t:8 * t + 1],
                                     lg_sb[:, 16 * t:16 * t + 16], axis=AX.X))
        T4(lambda e, t: e.tensor_single_scalar(
               t2b[:, 16 * t:16 * t + 16], lg_sb[:, 16 * t:16 * t + 16],
               t2a[:, 8 * t:8 * t + 1], op=ALU.is_ge))
        T4(lambda e, t: e.scalar_tensor_tensor(
               lg_sb[:, 16 * t:16 * t + 16], t2b[:, 16 * t:16 * t + 16],
               -1e30, lg_sb[:, 16 * t:16 * t + 16], op0=ALU.mult,
               op1=ALU.add))
        T4(lambda e, t: e.reduce_max(t2a[:, 8 * t + 1:8 * t + 2],
                                     lg_sb[:, 16 * t:16 * t + 16], axis=AX.X))
        T4(lambda e, t: e.tensor_sub(t2a[:, 8 * t + 2:8 * t + 3],
                                     t2a[:, 8 * t:8 * t + 1],
                                     t2a[:, 8 * t + 1:8 * t + 2]), inc=s_rtd)
        dv = s_rtd.n
        T4(lambda e, t: e.activation(t2a[:, 8 * t + 3:8 * t + 4],
                                     t2a[:, 8 * t + 2:8 * t + 3], AF.Sigmoid),
           eng="act", waits=[(s_rtd, dv)], inc=s_rta)
        av2 = s_rta.n
        T4(lambda e, t: e.tensor_scalar(t2a[:, 8 * t + 4:8 * t + 5],
                                        t2a[:, 8 * t + 3:8 * t + 4], -1.0,
                                        1.0, op0=ALU.mult, op1=ALU.add),
           waits=[(s_rta, av2)])
        T4(lambda e, t: e.tensor_single_scalar(
               lg_sb[:, 16 * t:16 * t + 16], lg_sb[:, 16 * t:16 * t + 16],
               t2a[:, 8 * t + 1:8 * t + 2], op=ALU.is_ge))
        T4(lambda e, t: e.scalar_tensor_tensor(
               comb_sb[:, 16 * t:16 * t + 16], t2b[:, 16 * t:16 * t + 16],
               t2a[:, 8 * t + 3:8 * t + 4], t2b[:, 16 * t:16 * t + 16],
               op0=ALU.mult, op1=ALU.bypass))
        T4(lambda e, t: e.scalar_tensor_tensor(
               comb_bf[:, 16 * t:16 * t + 16], lg_sb[:, 16 * t:16 * t + 16],
               t2a[:, 8 * t + 4:8 * t + 5], comb_sb[:, 16 * t:16 * t + 16],
               op0=ALU.mult, op1=ALU.add), inc=s_rtd)
        dv2 = s_rtd.n
        tw = claim(["p4"]) + [(s_rtd, dv2), (s_ri, 2)]
        for t in range(4):
            op("te", lambda e, t=t: e.transpose(
                   P4[0:16, t * 128:t * 128 + 128],
                   comb_bf[:, 16 * t:16 * t + 16], ident[:, :]),
               waits=(tw if t == 0 else ()), inc=(s_rt if t == 3 else None))
        trv = s_rt.n
        op("act", lambda e, t8=t8: e.copy(
               combT[0:16, t8 * 512:t8 * 512 + 512], P4[0:16, 0:512]),
           waits=[(s_rt, trv)], inc=s_rta)
        release(["p4"], s_rta)
        RT_REF[0] = s_rta.n

    slot_rel = {}
    ew_ready = {}

    def load_expert(i):
        sl = eslot[i % NSLOT]
        dsem = d_es[i % NSLOT]
        w = [(s_s1, X1_DONE)]
        if i % NSLOT in slot_rel:
            w.append((s_grp, slot_rel[i % NSLOT]))
        if i < 4:
            n, hf = i // 2, i % 2
            srcs = (wg_s[n, :, hf * 256:(hf + 1) * 256],
                    wu_s[n, :, hf * 256:(hf + 1) * 256],
                    wd_s[n, hf * 256:(hf + 1) * 256, :])
        else:
            ei = i - 4
            srcs = (wg_r[ei], wu_r[ei], wd_r[ei])
        dma("pool", sl[:, 0:2048].rearrange("p (kc m) -> p kc m", kc=8),
            srcs[0].rearrange("(kc p) m -> p kc m", p=128), dsem, waits=w)
        dma("pool", sl[:, 2048:4096].rearrange("p (kc m) -> p kc m", kc=8),
            srcs[1].rearrange("(kc p) m -> p kc m", p=128), dsem)
        dma("pool", sl[:, 4096:6144].rearrange("p (mc f) -> p mc f", mc=2),
            srcs[2].rearrange("(mc p) f -> p mc f", p=128), dsem)
        ew_ready[i] = (dsem, dsem.n)

    cr_ready = {}
    cb_gate = [0, 0]
    cb_base = [0]

    def ffn_group(gi):
        grp = GROUPS[gi]
        first, last = (gi == 0), (gi == len(GROUPS) - 1)
        routed_g = [ei for ei in grp if ei >= 4]
        cb_base[0] = s_cb.n
        for t8 in range(NFT):
            tok0 = t8 * 512
            if first:
                router_batch(t8)
            if last:
                tl_list = ([0, 1] if NFT > 1 else [0]) if t8 == 0 else (
                    [t8 + 1] if t8 + 1 < NFT else [])
                for tl in tl_list:
                    fas = d_fa2[tl % 2]
                    w = [(d_acc2[0], d_acc2[0].n), (d_acc2[1], d_acc2[1].n)]
                    if tl >= 2:
                        w.append((s_s2, 8 * (tl - 1)))
                    for fc in range(8):
                        dma("sync", acc_ld[:, (tl % 2) * 4096 + fc * 512:
                                           (tl % 2) * 4096 + fc * 512 + 512],
                            facc[fc, :, tl * 512:tl * 512 + 512], fas,
                            waits=w)
                    acc_ready[tl] = (fas, fas.n)
            if routed_g:
                nxt = t8 + 1 if t8 + 1 < NFT else None
                tgts = [t8] if t8 == 0 else ([] if nxt is None else [nxt])
                if t8 == 0 and NFT > 1:
                    tgts = [0, 1]
                for tl in tgts:
                    crs = d_cr2[tl % 2]
                    cw = [(s_rta, RT_REF[0]), (s_cb, cb_gate[tl % 2])]
                    cb_gate[tl % 2] = cb_base[0] + len(routed_g) * (tl + 1)
                    for j, ei in enumerate(grp):
                        if ei < 4:
                            continue
                        dma("sync",
                            comb_rowT[0:1, (tl % 2) * 2048 + j * 512:
                                      (tl % 2) * 2048 + j * 512 + 512],
                            combT[ei - 4:ei - 3, tl * 512:tl * 512 + 512],
                            crs, waits=cw)
                    for ei in routed_g:
                        cr_ready[(ei, tl)] = (crs, crs.n)

            for j, ei in enumerate(grp):
                sl = eslot[ei % NSLOT]
                routed = ei >= 4
                cbs = comb_bc[:, (j % 2) * 512:(j % 2) * 512 + 512]
                if routed:
                    cw = claim(["p7"]) + [cr_ready[(ei, t8)],
                                          (s_cbe, max(0, s_cbe.n - 1))]
                    op("te", lambda e, t8=t8, j=j: e.matmul(
                           P7[:, 0:512], ones1r[:, :],
                           comb_rowT[0:1, (t8 % 2) * 2048 + j * 512:
                                     (t8 % 2) * 2048 + j * 512 + 512],
                           start=True, stop=True),
                       waits=cw, inc=s_cb)
                    cbv = s_cb.n
                    op("act", lambda e, cbs=cbs: e.copy(cbs, P7[:, 0:512]),
                       waits=[(s_cb, cbv)], inc=s_cbe)
                    cbev = s_cbe.n
                    release(["p7"], s_cbe)
                for (dst, woff, smm, res) in ((P01, 0, s_g, ["r0", "r1"]),
                                              (P23, 2048, s_u, ["r2", "r3"])):
                    w0 = claim(res) + [ew_ready[ei],
                                       (s_s1, X1_DONE)]
                    for mc in range(2):
                        for i in range(8):
                            def mk(e, i=i, mc=mc, dst=dst, woff=woff, sl=sl,
                                   tok0=tok0):
                                return e.matmul(
                                    dst[:, mc * 512:mc * 512 + 512],
                                    sl[:, woff + i * 256 + mc * 128:
                                       woff + i * 256 + mc * 128 + 128],
                                    x_bf[:, i * S + tok0:i * S + tok0 + 512],
                                    start=(i == 0), stop=(i == 7))
                            op("te", mk,
                               waits=(w0 if (mc == 0 and i == 0) else ()),
                               inc=(smm if (mc == 1 and i == 7) else None))
                gv, uv = s_g.n, s_u.n
                op("act", lambda e: e.activation(silu_sb[:, :], P01[:, :],
                                                 AF.Silu),
                   waits=[(s_g, gv), (s_h, s_h.n)], inc=s_sil)
                sv_ = s_sil.n
                release(["r0", "r1"], s_sil)
                hsl = h_sb[:, j * 1024:j * 1024 + 1024]
                hw2 = [(s_u, uv), (s_dn, s_dn.n)]
                if routed:
                    for mc in range(2):
                        op("dve", lambda e, mc=mc, hsl=hsl, cbs=cbs:
                               e.scalar_tensor_tensor(
                                   hsl[:, mc * 512:mc * 512 + 512],
                                   P23[:, mc * 512:mc * 512 + 512], 1.0, cbs,
                                   op0=ALU.mult, op1=ALU.mult),
                           waits=(hw2 + [(s_cbe, cbev)] if mc == 0 else ()),
                           inc=(s_usc if mc == 1 else None), drain=(mc == 0))
                else:
                    op("dve", lambda e, hsl=hsl: e.tensor_copy(hsl, P23[:, :]),
                       waits=hw2, inc=s_usc, drain=True)
                uscv = s_usc.n
                release(["r2", "r3"], s_usc)
                op("dve", lambda e, hsl=hsl: e.tensor_mul(hsl, hsl,
                                                          silu_sb[:, :]),
                   waits=[(s_usc, uscv), (s_sil, sv_)], inc=s_h, drain=True)
            hv_all = s_h.n
            bank_ap = {"p5": P5[:, 0:512], "p6": P6[:, 0:512],
                       "p7": P7[:, 0:512],
                       "r0": P01[:, 0:512], "r1": P01[:, 512:1024],
                       "r2": P23[:, 0:512], "r3": P23[:, 512:1024]}
            for fcs, bres in (((0, 1, 2), ("p5", "p6", "p7")),
                              ((3, 4, 5), ("p5", "p6", "p7")),
                              ((6, 7), ("p5", "p6"))):
                nmm = len(fcs) * 2 * len(grp)
                w0 = [(s_h, hv_all)]
                for r in bres:
                    w0 += claim([r])
                im = 0
                for j, ei in enumerate(grp):
                    sl = eslot[ei % NSLOT]
                    for fi, fc in enumerate(fcs):
                        ps_ap = bank_ap[bres[fi]]
                        for mc in range(2):
                            def mk(e, sl=sl, fc=fc, mc=mc, ps_ap=ps_ap, j=j):
                                return e.matmul(
                                    ps_ap,
                                    sl[:, 4096 + mc * 1024 + fc * 128:
                                       4096 + mc * 1024 + fc * 128 + 128],
                                    h_sb[:, j * 1024 + mc * 512:
                                         j * 1024 + mc * 512 + 512],
                                    start=(j == 0 and mc == 0),
                                    stop=(j == len(grp) - 1 and mc == 1))
                            im += 1
                            op("te", mk, waits=(w0 if im == 1 else ()),
                               inc=(s_dn if im == nmm else None))
                dnv = s_dn.n
                for fi, fc in enumerate(fcs):
                    ps = bank_ap[bres[fi]]
                    if last:
                        asl = acc_ld[:, (t8 % 2) * 4096 + fc * 512:
                                     (t8 % 2) * 4096 + fc * 512 + 512]
                        xsl = x_bf[:, fc * S + tok0:fc * S + tok0 + 512]
                        op("dve", lambda e, ps=ps, asl=asl, xsl=xsl:
                               e.scalar_tensor_tensor(xsl, ps, 1.0, asl,
                                                      op0=ALU.mult,
                                                      op1=ALU.add),
                           waits=[(s_dn, dnv),
                                  acc_ready[t8]],
                           inc=s_s2, drain=True)
                        release([bres[fi]], s_s2)
                    else:
                        sgl = stage[:, (fc % 2) * 512:(fc % 2) * 512 + 512]
                        accs = d_acc2[fc % 2]
                        dw2 = [(s_dn, dnv), (accs, accs.n)]
                        if first:
                            xsl = x_bf[:, fc * S + tok0:fc * S + tok0 + 512]
                            op("dve", lambda e, ps=ps, sgl=sgl, xsl=xsl:
                                   e.scalar_tensor_tensor(sgl, ps, 1.0, xsl,
                                                          op0=ALU.mult,
                                                          op1=ALU.add),
                               waits=dw2, inc=s_dne, drain=True)
                        else:
                            op("act", lambda e, ps=ps, sgl=sgl: e.copy(
                                   sgl, ps),
                               waits=dw2, inc=s_dne, drain=True)
                        ev = s_dne.n
                        release([bres[fi]], s_dne)
                        dma("pool", facc[fc, :, tok0:tok0 + 512], sgl, accs,
                            waits=[(s_dne, ev)], accum=(not first))
            if last:
                s2v = s_s2.n
                sw = claim(["p5", "p6"])
                st_base = s_stat.n
                for j in range(8):
                    fc = j
                    xsl = x_bf[:, fc * S + tok0:fc * S + tok0 + 512]
                    op("te", lambda e, xsl=xsl, j=j: e.matmul(
                           P5[:, 0:512], ones128[:, :], xsl, start=(j == 0),
                           stop=(j == 7)),
                       waits=([(s_s2, s2v)] + sw if j == 0 else
                              [(s_s2, s2v)]),
                       inc=s_stat)
                sq_v = []
                for j in range(8):
                    fc = j
                    xsl = x_bf[:, fc * S + tok0:fc * S + tok0 + 512]
                    sqs = sq_bf[:, (j % 2) * 512:(j % 2) * 512 + 512]
                    w = [(s_s2, s2v)]
                    if j >= 2:
                        w.append((s_stat, st_base + 8 + (j - 2) + 1))
                    op("act", lambda e, xsl=xsl, sqs=sqs: e.activation(
                           sqs, xsl, AF.Square),
                       waits=w, inc=s_lna, drain=True)
                    sq_v.append(s_lna.n)
                for j in range(8):
                    sqs = sq_bf[:, (j % 2) * 512:(j % 2) * 512 + 512]
                    op("te", lambda e, j=j, sqs=sqs: e.matmul(
                           P6[:, 0:512], ones128[:, :], sqs, start=(j == 0),
                           stop=(j == 7)),
                       waits=[(s_lna, sq_v[j])], inc=s_stat)
                stv = s_stat.n
                op("dve", lambda e: e.tensor_single_scalar(
                       mu_bc[:, :], P5[:, :], 1.0 / D, op=ALU.mult),
                   waits=[(s_stat, stv)], drain=True)
                op("dve", lambda e: e.tensor_mul(var_t[:, :], mu_bc[:, :],
                                                 mu_bc[:, :]), drain=True)
                op("dve", lambda e: e.scalar_tensor_tensor(
                       var_t[:, :], P6[:, :], 1.0 / D, var_t[:, :],
                       op0=ALU.mult, op1=ALU.subtract), inc=s_lnd, drain=True)
                lv = s_lnd.n
                release(["p5", "p6"], s_lnd)
                op("act", lambda e: e.activation(var_t[:, :], var_t[:, :],
                                                 AF.Ln, bias=eps_sb[:, 0:1]),
                   waits=[(s_lnd, lv)], drain=True)
                op("act", lambda e: e.activation(r_bf[:, :], var_t[:, :],
                                                 AF.Exp, scale=-0.5),
                   inc=s_lna, drain=True)
                rv = s_lna.n
                for fc in range(8):
                    xsl = x_bf[:, fc * S + tok0:fc * S + tok0 + 512]
                    osl = out_st2[:, (fc % 2) * 512:(fc % 2) * 512 + 512]
                    ods = d_out2[fc % 2]
                    op("dve", lambda e, xsl=xsl: e.tensor_sub(
                           sq_bf[:, 0:512], xsl, mu_bc[:, :]),
                       waits=[(s_lna, rv), (ods, ods.n)],
                       drain=True)
                    op("dve", lambda e, osl=osl: e.tensor_mul(
                           osl, sq_bf[:, 0:512], r_bf[:, :]),
                       inc=s_x2, drain=True)
                    xv = s_x2.n
                    dma("sync", outT[fc * 128:fc * 128 + 128, tok0:tok0 + 512],
                        osl, ods, waits=[(s_x2, xv)])
        op("te", lambda e: e.nop(), inc=s_grp)
        for ei in grp:
            slot_rel[ei % NSLOT] = s_grp.n

    for i in GROUPS[0]:
        load_expert(i)
    acc_ready = {}
    for gi in range(len(GROUPS)):
        ffn_group(gi)
        if gi + 1 < len(GROUPS):
            for i in GROUPS[gi + 1]:
                load_expert(i)

    P["sync"].append(lambda e: e.wait_ge(d_out2[0].h, d_out2[0].n))
    P["sync"].append(lambda e: e.wait_ge(d_out2[1].h, d_out2[1].n))

    bcm = nc.Block()
    block = bcm.__enter__()

    def runner(eng):
        def f(e):
            for g in P[eng]:
                g(e)
        return f
    block.tensor(runner("te"))
    block.vector(runner("dve"))
    block.scalar(runner("act"))
    block.gpsimd(runner("pool"))
    block.sync(runner("sync"))
    bcm.__exit__(None, None, None)
    return nc


def kernel(L0_attn_x, L0_attn_alpha, L0_attn_Wq, L0_attn_bq, L0_attn_Wk,
           L0_attn_bk, L0_attn_Wv, L0_attn_bv, L0_attn_W_o, L0_attn_Wfc,
           L0_attn_bfc, L0_ffn_router_w, L0_ffn_wg_r, L0_ffn_wu_r,
           L0_ffn_wd_r, L0_ffn_wg_s, L0_ffn_wu_s, L0_ffn_wd_s):
    x = np.asarray(L0_attn_x, dtype=np.float32)
    B, S, _ = x.shape
    f32 = lambda a: np.ascontiguousarray(np.asarray(a, dtype=np.float32))
    xl = f32(x[:, -2:, :].reshape(B, 2 * D))
    alpha = f32(L0_attn_alpha).reshape(1, 1)
    b2 = lambda b: f32(np.asarray(b, dtype=np.float32).reshape(8, 128).T)
    woa = f32(L0_attn_W_o)
    wo2 = np.zeros((128, 1024), np.float32)
    for pp in range(8):
        wo2[0:64, pp * 128:pp * 128 + 64] = woa[2 * pp]
        wo2[64:128, pp * 128 + 64:pp * 128 + 128] = woa[2 * pp + 1]
    wo2 = f32(wo2)
    common = {"xl": xl, "alpha": alpha, "wq": f32(L0_attn_Wq),
              "wk": f32(L0_attn_Wk), "wv": f32(L0_attn_Wv),
              "wfc": f32(L0_attn_Wfc), "bq": b2(L0_attn_bq),
              "bk": b2(L0_attn_bk), "bfc": b2(L0_attn_bfc),
              "bv": b2(L0_attn_bv), "wo2": wo2, "rw": f32(L0_ffn_router_w),
              "wg_r": f32(L0_ffn_wg_r), "wu_r": f32(L0_ffn_wu_r),
              "wd_r": f32(L0_ffn_wd_r), "wg_s": f32(L0_ffn_wg_s),
              "wu_s": f32(L0_ffn_wu_s), "wd_s": f32(L0_ffn_wd_s)}
    nc = build_nc(S)
    in_maps = [dict(common, xT=np.ascontiguousarray(x[c].T)) for c in range(B)]
    res = run_bass_kernel_spmd(nc, in_maps, core_ids=list(range(B)))
    out = np.stack([np.asarray(r["outT"]).T for r in res.results], axis=0)
    return out.astype(np.float32)
